# version 18
# baseline (speedup 1.0000x reference)
import sys
import numpy as np

sys.path.insert(0, "/opt/trn_rl_repo")

import concourse.bass as bass
import concourse.mybir as mybir
from concourse.bass_utils import run_bass_kernel_spmd

F32 = mybir.dt.float32
BF16 = mybir.dt.bfloat16
AF = mybir.ActivationFunctionType
ALU = mybir.AluOpType
AX = mybir.AxisListType.X

NCORES = 8
B, S, D = 4, 4096, 2048
H, DH = 16, 128
NTOK = 2048
NST = 16
DC = 16
NSREP = 2
EPS = 1e-5
ISQDH = float(1.0 / np.sqrt(DH))

ENGINES = ("sync", "tensor", "vector", "scalar", "gpsimd")


class Sem:
    __slots__ = ("h", "n", "name")

    def __init__(self, h, name):
        self.h = h
        self.n = 0
        self.name = name


class Rec:
    def __init__(self):
        self.q = {e: [] for e in ENGINES}
        self.waited = {e: {} for e in ENGINES}
        self.ninstr = 0

    def do(self, eng, make, inc=None):
        if inc is not None:
            sem, amt = inc
            sem.n += amt
            h, a = sem.h, amt

            def fn(e, make=make, h=h, a=a):
                make(e).then_inc(h, a)
        else:
            def fn(e, make=make):
                make(e)
        self.q[eng].append(fn)
        self.ninstr += 1
        return inc[0].n if inc is not None else None

    def need(self, eng, sem, val=None):
        if val is None:
            val = sem.n
        if val <= 0:
            return
        w = self.waited[eng]
        if w.get(sem.name, -1) >= val:
            return
        w[sem.name] = val
        self.q[eng].append(lambda e, h=sem.h, v=val: e.wait_ge(h, v))
        self.ninstr += 1


class Build:
    def __init__(self, nc, rec, sems, t):
        self.nc = nc
        self.r = rec
        self.s = sems
        self.t = t
        self.uid = 0
        self.all_sems = [v for v in sems.values() if isinstance(v, Sem)]
        for v in sems.values():
            if isinstance(v, list):
                self.all_sems.extend(v)

    def nm(self, base):
        self.uid += 1
        return f"{base}_{self.uid}"

    def barrier(self):
        for e in ENGINES:
            if e == "gpsimd":
                continue
            for s_ in self.all_sems:
                self.r.need(e, s_, s_.n)

    def entry_wait(self, eng="tensor"):
        for e in (eng, "sync"):
            self.r.need(e, self.s["SV"])
            self.r.need(e, self.s["SA"])
            self.r.need(e, self.s["SP"])

    def ln_phase(self, src):
        r, s, t = self.r, self.s, self.t
        nc = self.nc
        from contextlib import ExitStack
        with ExitStack() as _stk:
            _e = _stk.enter_context
            ps_t = [_e(nc.psum_tensor(self.nm("ps_t"), [128, 512], BF16))
                    for _ in range(2)]
            xtile = [_e(nc.sbuf_tensor(self.nm("xt"), [128, 2048], F32))
                     for _ in range(2)]
            xln_tm = [_e(nc.sbuf_tensor(self.nm("xm"), [128, 2048], BF16))
                      for _ in range(2)]
            stats = [_e(nc.sbuf_tensor(self.nm("sst"), [128, 24], F32))
                     for _ in range(2)]
            mvs = [_e(nc.sbuf_tensor(self.nm("mvv"), [128, 2], F32))
                   for _ in range(2)]
            lnts = [_e(nc.sbuf_tensor(self.nm("lt"), [128, 3], F32))
                    for _ in range(2)]
            self.entry_wait("tensor")
            gmarks = []
            apply_done = [0, 0]
            tp_done = [0, 0]
            evac_done = [0, 0]
            for st in range(NST):
                par = st % 2
                xb = xtile[par]
                sxl = s["SXL"][par]
                r.need("sync", s["SA"], apply_done[par])
                ld = r.do("sync", lambda e, xb=xb, st=st, src=src: e.dma_start(
                    xb[:], src[st * 128:(st + 1) * 128, :]), inc=(sxl, 16))
                r.need("vector", sxl, ld)
                r.need("vector", s["SA"], apply_done[par])
                stt = stats[par]
                for i in range(4):
                    inc = (s["SV"], 1) if i == 3 else None
                    r.do("vector", lambda e, i=i, xb=xb, stt=stt: e.bn_stats(
                        stt[:, 6 * i:6 * i + 6], xb[:, 512 * i:512 * (i + 1)]),
                        inc=inc)
                r.need("vector", s["SV"])
                mv = mvs[par]
                r.do("vector", lambda e, mv=mv, stt=stt: e.bn_aggr(mv[:], stt[:]),
                     inc=(s["SV"], 1))
                r.need("vector", s["SV"])
                lnt = lnts[par]
                r.do("vector", lambda e, lnt=lnt, mv=mv: e.tensor_scalar_add(
                    lnt[:, 0:1], mv[:, 1:2], EPS), inc=(s["SV"], 1))
                r.need("scalar", s["SV"])
                r.do("scalar", lambda e, lnt=lnt: e.activation(
                    lnt[:, 0:1], lnt[:, 0:1], AF.Ln), inc=(s["SA"], 1))
                r.need("scalar", s["SA"])
                r.do("scalar", lambda e, lnt=lnt: e.activation(
                    lnt[:, 1:2], lnt[:, 0:1], AF.Exp, scale=-0.5),
                    inc=(s["SA"], 1))
                r.need("vector", s["SA"])
                r.do("vector", lambda e, lnt=lnt, mv=mv: e.scalar_tensor_tensor(
                    lnt[:, 2:3], mv[:, 0:1], -1.0, lnt[:, 1:2],
                    ALU.mult, ALU.mult), inc=(s["SV"], 1))
                xm = xln_tm[par]
                r.need("scalar", s["SV"])
                r.need("scalar", s["SP"], tp_done[par])
                av = r.do("scalar", lambda e, xm=xm, xb=xb, lnt=lnt: e.activation(
                    xm[:], xb[:], AF.Identity, bias=lnt[:, 2:3],
                    scale=lnt[:, 1:2]), inc=(s["SA"], 1))
                apply_done[par] = av
                r.need("tensor", s["SA"], av)
                for g4 in range(4):
                    p2 = g4 % 2
                    pst = ps_t[p2]
                    r.need("tensor", s["SV"], evac_done[p2])
                    for k in range(4):
                        dc = g4 * 4 + k
                        inc = (s["SP"], 1) if k == 3 else None
                        r.do("tensor", lambda e, pst=pst, k=k, xm=xm, dc=dc:
                             e.transpose(pst[:, k * 128:(k + 1) * 128],
                                         xm[:, dc * 128:(dc + 1) * 128],
                                         t["ident_bf"][:]), inc=inc)
                    tpv = s["SP"].n
                    if g4 == 3:
                        tp_done[par] = tpv
                    r.need("vector", s["SP"], tpv)
                    dst = t["bigA"][:].rearrange("p (dc n) -> p dc n", n=NTOK)[
                        :, g4 * 4:(g4 + 1) * 4, st * 128:(st + 1) * 128]
                    srcap = pst[:].rearrange("p (k n) -> p k n", n=128)
                    ev = r.do("vector", lambda e, dst=dst, srcap=srcap:
                              e.tensor_copy(dst, srcap), inc=(s["SV"], 1))
                    evac_done[p2] = ev
                if st % 4 == 3:
                    gmarks.append(s["SV"].n)
        return gmarks

    def load_weight(self, codes, scales, mid=None, defer_b=False):
        r, s, t = self.r, self.s, self.t
        if not hasattr(self, "wring_free"):
            self.wring_free = [0, 0, 0]
            self.wring_a = 0
            self.srep_free = [0] * NSREP
        a = self.wring_a
        b = (a + 1) % 3
        self.wring_a = (a + 2) % 3
        halves = [(t["bigW"], 0), (t["bigW"], 8 * 2048), (t["pbuf"], 0)]

        def wsel(dc, c0, c1, a=a, b=b, halves=halves):
            h = a if dc < 8 else b
            ten, off = halves[h]
            base = off + (dc % 8) * 2048
            return ten[:, base + c0: base + c1]

        ready = []

        def emit_half(h, src_col, dclo):
            r.need("gpsimd", s["SP"], self.wring_free[h])
            ten, off = halves[h]
            m = 0
            for part in range(4):
                m = r.do("gpsimd", lambda e, ten=ten, off=off, part=part,
                         src_col=src_col, codes=codes: e.dma_start(
                             ten[:, off + part * 4096: off + (part + 1) * 4096],
                             codes[:, src_col + part * 4096:
                                   src_col + (part + 1) * 4096]),
                         inc=(s["SWH"][h], 16))
            ready.append((s["SWH"][h], m))
            if scales is not None:
                for dc in range(dclo, dclo + 8):
                    par = dc % NSREP
                    sr = t["srep"][par]
                    ssr = s["SSR"][par]
                    r.need("gpsimd", s["SV"], self.srep_free[par])
                    srcap = scales[dc:dc + 1, :, :].rearrange(
                        "dc g f -> g dc f")
                    srcap = srcap.unsqueeze(1).broadcast_to((4, 32, 1, 2048))
                    sm = 0
                    for ai in range(4):
                        sm = r.do("gpsimd", lambda e, sr=sr, srcap=srcap,
                                  ai=ai: e.dma_start(
                                      sr[ai * 32:(ai + 1) * 32, :],
                                      srcap[ai:ai + 1]), inc=(ssr, 16))
                    r.need("vector", ssr, sm)
                    r.need("vector", s["SWH"][h], m)
                    dst = wsel(dc, 0, 2048)
                    ev = r.do("vector", lambda e, dst=dst, sr=sr:
                              e.tensor_tensor(dst, dst, sr[:], ALU.mult),
                              inc=(s["SV"], 1))
                    self.srep_free[par] = ev

        emit_half(a, 0, 0)

        def finish(mid2=None):
            if mid2 is not None:
                mid2()
            emit_half(b, 8 * 2048, 8)
            if scales is not None:
                ready.append((s["SV"], s["SV"].n))
            return ready

        if defer_b:
            return wsel, ready, (a, b), finish
        finish(mid)
        return wsel, ready, (a, b)

    def wring_release(self, ab):
        a, b = ab
        self.wring_free[a] = self.s["SP"].n
        self.wring_free[b] = self.s["SP"].n

    def proj_fm(self, ps_p, stage, gmarks, wsel, ready, bcol_ap, dst_dram,
                dst_row0, share):
        r, s, t = self.r, self.s, self.t
        psfree = share["ps"]
        stfree = share["stg"]
        k = 0
        for sem, val in ready:
            r.need("tensor", sem, val)
        for g in range(4):
            r.need("tensor", s["SV"], gmarks[g])
            for f in range(DC):
                ps = ps_p[k % 3]
                if psfree[k % 3] is not None:
                    r.need("tensor", psfree[k % 3][0], psfree[k % 3][1])
                for dc in range(DC):
                    inc = (s["SP"], 1) if dc == DC - 1 else None
                    wap = wsel(dc, f * 128, (f + 1) * 128)
                    r.do("tensor", lambda e, ps=ps, dc=dc, g=g, wap=wap:
                         e.matmul(
                             ps[:], wap,
                             t["bigA"][:, dc * 2048 + g * 512:
                                       dc * 2048 + (g + 1) * 512],
                             start=(dc == 0), stop=(dc == DC - 1)),
                         inc=inc)
                mm = s["SP"].n
                slot = k % 4
                stg = stage[slot]
                sst = s["STG"][slot]
                r.need("scalar", s["SP"], mm)
                r.need("scalar", sst, stfree[slot])
                ev = r.do("scalar", lambda e, stg=stg, ps=ps, f=f:
                          e.activation(stg[:], ps[:], AF.Identity,
                                       bias=bcol_ap[:, f:f + 1]),
                          inc=(s["SA"], 1))
                psfree[k % 3] = (s["SA"], ev)
                r.need("sync", s["SA"], ev)
                st_ = r.do("sync", lambda e, stg=stg, dst_dram=dst_dram,
                           f=f, g=g: e.dma_start(dst_dram[
                               dst_row0 + f * 128: dst_row0 + (f + 1) * 128,
                               g * 512:(g + 1) * 512], stg[:]),
                           inc=(sst, 16))
                stfree[slot] = st_
                k += 1
        return [(s["STG"][i], s["STG"][i].n) for i in range(4)]

    def proj_tm(self, ps_p, stage, gmarks, wsel, ready, brow_dram, dst_dram,
                dst_row0, share, ln_marks=None):
        r, s, t = self.r, self.s, self.t
        psfree = share["ps"]
        stfree = share["stg"]
        k = 0
        for sem, val in ready:
            r.need("tensor", sem, val)
        from contextlib import ExitStack
        with ExitStack() as _stk:
            brow = _stk.enter_context(self.nc.sbuf_tensor(
                self.nm("brow"), [1, 2048], BF16))
            ones_row = _stk.enter_context(self.nc.sbuf_tensor(
                self.nm("onr"), [1, 128], BF16))
            brow_ap = brow[:]
            if ln_marks is not None:
                r.need("gpsimd", s["SA"], ln_marks[0])
                r.need("gpsimd", s["SV"], ln_marks[1])
            else:
                r.need("gpsimd", s["SP"])
            bm = r.do("gpsimd", lambda e, brow=brow, brow_dram=brow_dram:
                      e.dma_start(brow[:], brow_dram), inc=(s["SCNW"], 16))
            r.need("tensor", s["SCNW"], bm)
            r.need("vector", s["SA"])
            r.do("vector", lambda e, ones_row=ones_row: e.memset(
                ones_row[:], 1.0), inc=(s["SV"], 1))
            r.need("tensor", s["SV"])
            for n in range(DC):
                r.need("tensor", s["SV"], gmarks[n // 4])
                for ft in range(4):
                    ps = ps_p[k % 3]
                    if psfree[k % 3] is not None:
                        r.need("tensor", psfree[k % 3][0], psfree[k % 3][1])
                    for dc in range(DC):
                        wap = wsel(dc, ft * 512, (ft + 1) * 512)
                        r.do("tensor", lambda e, ps=ps, dc=dc, n=n, wap=wap:
                             e.matmul(ps[:],
                                      t["bigA"][:, dc * 2048 + n * 128:
                                                dc * 2048 + (n + 1) * 128],
                                      wap,
                                      start=(dc == 0), stop=False))
                    r.do("tensor", lambda e, ps=ps, ft=ft, brow_ap=brow_ap,
                         ones_row=ones_row:
                         e.matmul(ps[:], ones_row[:],
                                  brow_ap[:, ft * 512:(ft + 1) * 512],
                                  start=False, stop=True), inc=(s["SP"], 1))
                    mm = s["SP"].n
                    slot = k % 4
                    stg = stage[slot]
                    sst = s["STG"][slot]
                    r.need("vector", s["SP"], mm)
                    r.need("vector", sst, stfree[slot])
                    ev = r.do("vector", lambda e, stg=stg, ps=ps: e.tensor_copy(
                        stg[:], ps[:]), inc=(s["SV"], 1))
                    psfree[k % 3] = (s["SV"], ev)
                    r.need("sync", s["SV"], ev)
                    st_ = r.do("sync", lambda e, stg=stg, dst_dram=dst_dram,
                               n=n, ft=ft: e.dma_start(dst_dram[
                                   dst_row0 + n * 128: dst_row0 + (n + 1) * 128,
                                   ft * 512:(ft + 1) * 512], stg[:]),
                               inc=(sst, 16))
                    stfree[slot] = st_
                    k += 1
        return [(s["STG"][i], s["STG"][i].n) for i in range(4)]

    def attention(self, causal, q_marks, v2b_ap):
        r, s, t = self.r, self.s, self.t
        self.entry_wait("tensor")
        for sem, val in q_marks:
            r.need("sync", sem, val)
            r.need("scalar", sem, val)
        nc = self.nc
        from contextlib import ExitStack
        with ExitStack() as _stk:
            _e = _stk.enter_context
            ps_S = [_e(nc.psum_tensor(self.nm("ps_S"), [128, 1024], F32))
                    for _ in range(2)]
            ps_O = _e(nc.psum_tensor(self.nm("ps_O"), [128, 512], F32))
            ps_sums = _e(nc.psum_tensor(self.nm("ps_su"), [128, 4], F32))
            ps_T = _e(nc.psum_tensor(self.nm("ps_T"), [128, 512], BF16))
            V4sb = _e(nc.sbuf_tensor(self.nm("V2"), [128, 32 * 256], BF16))
            K_h = [_e(nc.sbuf_tensor(self.nm("kh"), [128, 4096], BF16))
                   for _ in range(2)]
            Qg = [_e(nc.sbuf_tensor(self.nm("qg"), [128, 512], BF16))
                  for _ in range(2)]
            ptil = [_e(nc.sbuf_tensor(self.nm("pt"), [128, 1024], BF16))
                    for _ in range(2)]
            Otm = _e(nc.sbuf_tensor(self.nm("Otm"), [128, 512], BF16))
            mtile = [Otm]
            recip = _e(nc.sbuf_tensor(self.nm("rcp"), [128, 4], F32))

            v2bufs = [V4sb[:], v2b_ap]
            sp0 = s["SP"].n
            st8 = {"kq_free": [0, 0], "v4_free": [sp0, sp0], "pend": None,
                   "pv_mark": [0, 0], "swar": [0, 0], "evT": 0,
                   "mt_free": [0, 0], "mt_par": 0, "v4m": [0, 0],
                   "v2cur": None, "qg_free": [0, 0]}

            def pv_block(jlist, reg, qg, hh, expm, epoch_first, final=False):
                vb, vpar_ = st8["v2cur"]
                r.need("tensor", s["SV4"][vpar_], st8["v4m"][vpar_])
                pairs = []
                for k, j in enumerate(jlist):
                    ls = j % 16
                    for qs in range(4):
                        if causal and ls > 4 * qg + qs:
                            continue
                        pairs.append((k, j, qs))
                for pi, (k, j, qs) in enumerate(pairs):
                    ls, sec = j % 16, j // 16
                    is_last = final and (pi == len(pairs) - 1)
                    if True:
                        stq = 4 * qg + qs
                        masked = causal and (ls == stq)
                        lhs_t = None
                        if masked:
                            mp = 0
                            mt = mtile[mp]
                            r.need("vector", s["SA"], expm)
                            r.need("vector", s["SP"], st8["mt_free"][mp])
                            r.do("vector", lambda e, mt=mt, reg=reg, k=k, qs=qs,
                                 sec=sec: e.tensor_tensor(
                                     mt[:, 0:128],
                                     ptil[reg][:, k * 512 + qs * 128:
                                               k * 512 + (qs + 1) * 128],
                                     t["mask_sec"][sec][:], ALU.mult),
                                 inc=(s["SV"], 1))
                            r.need("tensor", s["SV"], s["SV"].n)
                            lhs_t = (mt, mp)
                        r.need("tensor", s["SA"], expm)
                        stO = epoch_first[0]
                        epoch_first[0] = False

                        def lhs(e, lhs_t=lhs_t, reg=reg, k=k, qs=qs):
                            if lhs_t is not None:
                                return lhs_t[0][:, 0:128]
                            return ptil[reg][:, k * 512 + qs * 128:
                                             k * 512 + (qs + 1) * 128]

                        r.do("tensor", lambda e, lhs=lhs, j=j, qs=qs, hh=hh,
                             vb=vb, stO=stO, is_last=is_last: e.matmul(
                                 ps_O[:, qs * 128:(qs + 1) * 128], lhs(e),
                                 vb[:, j * 256 + hh * 128:
                                    j * 256 + (hh + 1) * 128],
                                 start=stO, stop=is_last))
                        stSu = epoch_first[1]
                        epoch_first[1] = False
                        mm = r.do("tensor", lambda e, lhs=lhs, qs=qs, stSu=stSu,
                                  is_last=is_last:
                                  e.matmul(ps_sums[:, qs:qs + 1], lhs(e),
                                           t["ones_col"][:], start=stSu,
                                           stop=is_last), inc=(s["SP"], 1))
                        if lhs_t is not None:
                            st8["mt_free"][lhs_t[1]] = mm

            for u in range(8):
                vpar = u % 2
                vbuf = v2bufs[vpar]
                st8["v2cur"] = (vbuf, vpar)
                r.need("scalar", s["SP"], st8["v4_free"][vpar])
                if u == 0:
                    r.need("scalar", s["SCV"])
                    r.need("sync", s["SCK"])
                for sec in range(2):
                    for ci in range(4):
                        row0 = ci * 1024 + sec * 512
                        vsrc = t["kvf_V"][row0:row0 + 512,
                                          u * 256:(u + 1) * 256]
                        vsrc = vsrc.rearrange("(ls p) f -> p ls f", p=128)
                        c0 = (sec * 16 + ci * 4) * 256
                        vdst = vbuf[:, c0:c0 + 4 * 256]
                        vdst = vdst.rearrange("p (ls f) -> p ls f", f=256)
                        st8["v4m"][vpar] = r.do(
                            "scalar", lambda e, vdst=vdst, vsrc=vsrc:
                            e.dma_start(vdst, vsrc), inc=(s["SV4"][vpar], 16))
                for hh in range(2):
                    h = 2 * u + hh
                    par = hh % 2
                    kh = K_h[par]
                    skq = s["SKQ"][par]
                    r.need("sync", s["SP"], st8["kq_free"][par])
                    qm_ = 0
                    ci_, hp_ = h // 4, h % 4
                    for sec in range(2):
                        row0 = ci_ * 1024 + sec * 512 + hp_ * 128
                        qm_ = r.do("sync", lambda e, kh=kh, sec=sec, row0=row0:
                                   e.dma_start(
                                       kh[:, sec * 2048:(sec + 1) * 2048],
                                       t["kvf_K"][row0:row0 + 128, :]),
                                   inc=(skq, 16))
                    r.need("tensor", skq, qm_)
                    for qg in range(4):
                        qsl = qg % 2
                        r.need("sync", s["SP"], st8["qg_free"][qsl])
                        qgm = r.do("sync", lambda e, qsl=qsl, h=h, qg=qg:
                                   e.dma_start(
                                       Qg[qsl][:],
                                       t["q_dram"][h * 128:(h + 1) * 128,
                                                   qg * 512:(qg + 1) * 512]),
                                   inc=(s["SQG"][qsl], 16))
                        r.need("tensor", s["SQG"][qsl], qgm)
                        epoch_first = [True, True]
                        r.need("tensor", s["SV"], s["SV"].n)
                        if causal:
                            included = [j for j in range(32)
                                        if (j % 16) <= 4 * qg + 3]
                        else:
                            included = list(range(32))
                        regions = [included[i:i + 2]
                                   for i in range(0, len(included), 2)]
                        pend_pv = None
                        for ri, jgrp in enumerate(regions):
                            reg = ri % 2
                            r.need("tensor", s["SA"], st8["swar"][reg])
                            psS = ps_S[reg]
                            qga = Qg[qsl]
                            for slot, j in enumerate(jgrp):
                                r.do("tensor", lambda e, psS=psS, slot=slot,
                                     kh=kh, qga=qga, j=j: e.matmul(
                                         psS[:, slot * 512:(slot + 1) * 512],
                                         kh[:, j * 128:(j + 1) * 128],
                                         qga[:],
                                         start=True, stop=True),
                                     inc=(s["SP"], 1))
                            nsl = len(jgrp)
                            r.need("scalar", s["SP"], s["SP"].n)
                            r.need("scalar", s["SP"], st8["pv_mark"][reg])
                            pt = ptil[reg]
                            em = r.do("scalar", lambda e, psS=psS, pt=pt,
                                      nsl=nsl: e.activation(
                                          pt[:, 0:nsl * 512],
                                          psS[:, 0:nsl * 512],
                                          AF.Exp, scale=ISQDH),
                                      inc=(s["SA"], 1))
                            st8["swar"][reg] = em
                            if st8["pend"] is not None:
                                st8["pend"]()
                                st8["pend"] = None
                            if pend_pv is not None:
                                pv_block(*pend_pv)
                                st8["pv_mark"][pend_pv[1]] = s["SP"].n
                            pend_pv = (jgrp, reg, qg, hh, em, epoch_first)
                        if pend_pv is not None:
                            pv_block(*pend_pv, final=True)
                            st8["pv_mark"][pend_pv[1]] = s["SP"].n
                            pend_pv = None
                        st8["qg_free"][qsl] = s["SP"].n
                        r.need("vector", s["SP"], s["SP"].n)
                        r.do("vector", lambda e: e.reciprocal(
                            recip[:], ps_sums[:, 0:4]), inc=(s["SV"], 1))
                        r.need("vector", s["SV"])
                        for qs in range(4):
                            inc = (s["SV"], 1) if qs == 3 else None
                            r.do("vector", lambda e, qs=qs: e.tensor_scalar(
                                Otm[:, qs * 128:(qs + 1) * 128],
                                ps_O[:, qs * 128:(qs + 1) * 128],
                                recip[:, qs:qs + 1], None, ALU.mult), inc=inc)
                        nd = s["SV"].n

                        def flushT(h=h, qg=qg, nd=nd):
                            r.need("tensor", s["SV"], nd)
                            r.need("tensor", s["SV"], st8["evT"])
                            for qs in range(4):
                                inc = (s["SP"], 1) if qs == 3 else None
                                r.do("tensor", lambda e, qs=qs: e.transpose(
                                    ps_T[:, qs * 128:(qs + 1) * 128],
                                    Otm[:, qs * 128:(qs + 1) * 128],
                                    t["ident_bf"][:]), inc=inc)
                            tm = s["SP"].n
                            r.need("vector", s["SP"], tm)
                            st8["evT"] = r.do(
                                "vector", lambda e, h=h, qg=qg: e.tensor_copy(
                                    t["bigA"][:, h * 2048 + qg * 512:
                                              h * 2048 + (qg + 1) * 512],
                                    ps_T[:]), inc=(s["SV"], 1))

                        st8["pend"] = flushT
                    st8["kq_free"][par] = s["SP"].n
                st8["v4_free"][vpar] = s["SP"].n
            if st8["pend"] is not None:
                st8["pend"]()
                st8["pend"] = None

    def oproj_phase(self, wsel, ready, brow_dram, src, dst):
        r, s, t = self.r, self.s, self.t
        self.entry_wait("tensor")
        for sem, val in ready:
            r.need("tensor", sem, val)
        nc = self.nc
        from contextlib import ExitStack
        with ExitStack() as _stk:
            _e = _stk.enter_context
            ps_p = [_e(nc.psum_tensor(self.nm("ps_o"), [128, 512], F32))
                    for _ in range(2)]
            brow = _e(nc.sbuf_tensor(self.nm("brow"), [1, 2048], BF16))
            ones_row = _e(nc.sbuf_tensor(self.nm("onr"), [1, 128], BF16))
            brow_ap = brow[:]
            r.need("gpsimd", s["SP"])
            bm = r.do("gpsimd", lambda e, brow=brow, brow_dram=brow_dram:
                      e.dma_start(brow[:], brow_dram), inc=(s["SCNW"], 16))
            r.need("tensor", s["SCNW"], bm)
            r.need("vector", s["SA"])
            r.do("vector", lambda e, ones_row=ones_row: e.memset(
                ones_row[:], 1.0), inc=(s["SV"], 1))
            r.need("tensor", s["SV"])
            xtile = [_e(nc.sbuf_tensor(self.nm("oxt"), [128, 2048], F32))
                     for _ in range(2)]
            xnew = [_e(nc.sbuf_tensor(self.nm("oxn"), [128, 2048], F32))
                    for _ in range(2)]
            ps_free = [0, 0]
            x_free = [0, 0]
            xn_free = [0, 0]
            for st in range(NST):
                par = st % 2
                xb = xtile[par]
                sxl = s["SXL"][par]
                r.need("sync", s["SV"], x_free[par])
                xl = r.do("sync", lambda e, xb=xb, st=st, src=src: e.dma_start(
                    xb[:], src[st * 128:(st + 1) * 128, :]), inc=(sxl, 16))
                xn = xnew[par]
                sxn = s["SXN"][par]
                r.need("vector", sxn, xn_free[par])
                for dt_ in range(4):
                    ps = ps_p[dt_ % 2]
                    r.need("tensor", s["SV"], ps_free[dt_ % 2])
                    for f in range(DC):
                        wap = wsel(f, dt_ * 512, (dt_ + 1) * 512)
                        r.do("tensor", lambda e, ps=ps, f=f, st=st, wap=wap:
                             e.matmul(ps[:],
                                      t["bigA"][:, f * 2048 + st * 128:
                                                f * 2048 + (st + 1) * 128],
                                      wap,
                                      start=(f == 0), stop=False))
                    r.do("tensor", lambda e, ps=ps, dt_=dt_, brow_ap=brow_ap,
                         ones_row=ones_row:
                         e.matmul(ps[:], ones_row[:],
                                  brow_ap[:, dt_ * 512:(dt_ + 1) * 512],
                                  start=False, stop=True), inc=(s["SP"], 1))
                    mm = s["SP"].n
                    r.need("vector", s["SP"], mm)
                    r.need("vector", sxl, xl)
                    ev = r.do("vector", lambda e, xn=xn, ps=ps, xb=xb, dt_=dt_:
                              e.tensor_tensor(xn[:, dt_ * 512:(dt_ + 1) * 512],
                                              ps[:],
                                              xb[:, dt_ * 512:(dt_ + 1) * 512],
                                              ALU.add), inc=(s["SV"], 1))
                    ps_free[dt_ % 2] = ev
                x_free[par] = s["SV"].n
                r.need("sync", s["SV"], s["SV"].n)
                stm = r.do("sync", lambda e, xn=xn, st=st, dst=dst: e.dma_start(
                    dst[st * 128:(st + 1) * 128, :], xn[:]), inc=(sxn, 16))
                xn_free[par] = stm
        return [(s["SXN"][i], s["SXN"][i].n) for i in range(2)]

    def ffn_phase(self, wsel, ready, src, dst, absr, gmarks):
        r, s, t = self.r, self.s, self.t
        self.entry_wait("tensor")
        for sem, val in ready:
            r.need("tensor", sem, val)
        a_sb, b_sb, r_sb = absr
        nc = self.nc
        from contextlib import ExitStack
        with ExitStack() as _stk:
            _e = _stk.enter_context
            ps_r = _e(nc.psum_tensor(self.nm("ps_r"), [128, 8], F32))
            ps_wdT = _e(nc.psum_tensor(self.nm("ps_wdT"), [8, 128], BF16))
            ps_wrep = _e(nc.psum_tensor(self.nm("ps_wrp"), [128, 128], F32))
            ps_ax = _e(nc.psum_tensor(self.nm("ps_ax"), [128, 128], F32))
            ps_f = [_e(nc.psum_tensor(self.nm("ps_f"), [128, 512], F32))
                    for _ in range(4)]
            xtile = [_e(nc.sbuf_tensor(self.nm("fxt"), [128, 2048], F32))
                     for _ in range(2)]
            xnew = [_e(nc.sbuf_tensor(self.nm("fxn"), [128, 2048], F32))]
            top = _e(nc.sbuf_tensor(self.nm("top"), [128, 8], F32))
            t1s = _e(nc.sbuf_tensor(self.nm("t1s"), [128, 8], F32))
            t2s = _e(nc.sbuf_tensor(self.nm("t2s"), [128, 8], F32))
            mskd = _e(nc.sbuf_tensor(self.nm("mskd"), [128, 8], F32))
            td1 = _e(nc.sbuf_tensor(self.nm("td1"), [128, 8], F32))
            wdense = _e(nc.sbuf_tensor(self.nm("wd"), [128, 8], BF16))
            wdT_sb = _e(nc.sbuf_tensor(self.nm("wdT"), [8, 128], BF16))
            wrep_sb = _e(nc.sbuf_tensor(self.nm("wrp"), [128, 128], BF16))
            g_sb = [_e(nc.sbuf_tensor(self.nm("gsb"), [128, 128], BF16))
                    for _ in range(2)]
            exp8 = _e(nc.sbuf_tensor(self.nm("exp8"), [8, 128], BF16))
            r.need("gpsimd", s["SP"])
            e8m = r.do("gpsimd", lambda e, exp8=exp8: e.dma_start(
                exp8[:], t["exp8_dram"]), inc=(s["SCNW"], 16))
            r.need("tensor", s["SCNW"], e8m)
            ps_free = [0, 0, 0, 0]
            x_free = [0, 0]
            xn_free = [0]
            st9 = {"g_free": [0, 0], "r_read": 0, "wdt_free": 0,
                   "wrep_free": 0, "ax_free": 0, "wdense_free": 0,
                   "g_mark": [0, 0], "xl": [0, 0]}

            def x_load(n):
                par = n % 2
                xb = xtile[par]
                sxl = s["SXL"][par]
                r.need("sync", s["SV"], x_free[par])
                st9["xl"][par] = r.do(
                    "sync", lambda e, xb=xb, n=n, src=src: e.dma_start(
                        xb[:], src[n * 128:(n + 1) * 128, :]), inc=(sxl, 16))

            def router_ax(n):
                r.need("tensor", s["SV"], gmarks[n // 4])
                r.need("tensor", s["SV"], st9["r_read"])
                for dc in range(DC):
                    r.do("tensor", lambda e, dc=dc, n=n: e.matmul(
                        ps_r[:],
                        t["bigA"][:, dc * 2048 + n * 128:
                                  dc * 2048 + (n + 1) * 128],
                        r_sb[:, dc * 8:(dc + 1) * 8],
                        start=(dc == 0), stop=(dc == DC - 1)),
                        inc=(s["SP"], 1) if dc == DC - 1 else None)
                rmm = s["SP"].n
                r.need("vector", s["SP"], rmm)
                r.do("vector", lambda e: e.reduce_max(
                    top[:, 0:1], ps_r[:], axis=AX), inc=(s["SV"], 1))
                r.need("vector", s["SV"])
                r.do("vector", lambda e: e.tensor_scalar(
                    t1s[:], ps_r[:], top[:, 0:1], None, ALU.is_equal),
                    inc=(s["SV"], 1))
                r.need("vector", s["SV"])
                r.do("vector", lambda e: e.scalar_tensor_tensor(
                    mskd[:], t1s[:], -1e30, ps_r[:], ALU.mult, ALU.add),
                    inc=(s["SV"], 1))
                r.need("vector", s["SV"])
                r.do("vector", lambda e: e.reduce_max(
                    top[:, 1:2], mskd[:], axis=AX), inc=(s["SV"], 1))
                r.need("vector", s["SV"])
                r.do("vector", lambda e: e.tensor_scalar(
                    t2s[:], mskd[:], top[:, 1:2], None, ALU.is_equal),
                    inc=(s["SV"], 1))
                r.do("vector", lambda e: e.tensor_sub(
                    top[:, 2:3], top[:, 1:2], top[:, 0:1]), inc=(s["SV"], 1))
                r.need("scalar", s["SV"])
                r.do("scalar", lambda e: e.activation(
                    top[:, 3:4], top[:, 2:3], AF.Exp), inc=(s["SA"], 1))
                r.need("vector", s["SA"])
                r.do("vector", lambda e: e.tensor_scalar_add(
                    top[:, 4:5], top[:, 3:4], 1.0), inc=(s["SV"], 1))
                r.need("vector", s["SV"])
                r.do("vector", lambda e: e.reciprocal(
                    top[:, 5:6], top[:, 4:5]), inc=(s["SV"], 1))
                r.need("vector", s["SV"])
                r.do("vector", lambda e: e.tensor_mul(
                    top[:, 6:7], top[:, 3:4], top[:, 5:6]), inc=(s["SV"], 1))
                r.do("vector", lambda e: e.tensor_scalar(
                    td1[:], t1s[:], top[:, 5:6], None, ALU.mult),
                    inc=(s["SV"], 1))
                r.need("vector", s["SV"])
                r.need("vector", s["SP"], st9["wdense_free"])
                r.do("vector", lambda e: e.scalar_tensor_tensor(
                    wdense[:], t2s[:], top[:, 6:7], td1[:],
                    ALU.mult, ALU.add), inc=(s["SV"], 1))
                st9["r_read"] = s["SV"].n

            def gating_pe(n):
                r.need("tensor", s["SV"], st9["ax_free"])
                for dc in range(DC):
                    r.do("tensor", lambda e, dc=dc, n=n: e.matmul(
                        ps_ax[:],
                        a_sb[:, dc * 128:(dc + 1) * 128],
                        t["bigA"][:, dc * 2048 + n * 128:
                                  dc * 2048 + (n + 1) * 128],
                        start=(dc == 0), stop=(dc == DC - 1)),
                        inc=(s["SP"], 1) if dc == DC - 1 else None)
                wd = st9["r_read"]
                r.need("tensor", s["SV"], wd)
                r.need("tensor", s["SV"], st9["wdt_free"])
                r.do("tensor", lambda e: e.transpose(
                    ps_wdT[:], wdense[:], t["ident_bf"][:]), inc=(s["SP"], 1))
                st9["wdense_free"] = s["SP"].n
                r.need("vector", s["SP"], s["SP"].n)
                r.do("vector", lambda e: e.tensor_copy(wdT_sb[:], ps_wdT[:]),
                     inc=(s["SV"], 1))
                st9["wdt_free"] = s["SV"].n
                r.need("tensor", s["SV"], st9["wdt_free"])
                r.need("tensor", s["SV"], st9["wrep_free"])
                r.do("tensor", lambda e: e.matmul(
                    ps_wrep[:], exp8[:], wdT_sb[:],
                    start=True, stop=True), inc=(s["SP"], 1))
                wrepm = s["SP"].n
                r.need("vector", s["SP"], wrepm)
                r.do("vector", lambda e: e.tensor_copy(wrep_sb[:], ps_wrep[:]),
                     inc=(s["SV"], 1))
                st9["wrep_free"] = s["SV"].n
                gp = n % 2
                gsb = g_sb[gp]
                r.need("vector", s["SV"], s["SV"].n)
                r.need("vector", s["SP"], st9["g_free"][gp])
                r.do("vector", lambda e, gsb=gsb: e.tensor_tensor(
                    gsb[:], ps_ax[:], wrep_sb[:], ALU.mult), inc=(s["SV"], 1))
                st9["g_mark"][gp] = s["SV"].n
                st9["ax_free"] = s["SV"].n

            x_load(0)
            router_ax(0)
            gating_pe(0)
            for n in range(NST):
                par = n % 2
                xb = xtile[par]
                if n + 1 < NST:
                    x_load(n + 1)
                    router_ax(n + 1)
                gp = n % 2
                gsb = g_sb[gp]
                xn = xnew[0]
                sxn = s["SXN"][0]
                r.need("vector", sxn, xn_free[0])
                for ot in range(4):
                    ps = ps_f[ot]
                    r.need("tensor", s["SV"], ps_free[ot])
                    for dc in range(DC):
                        wap = wsel(dc, ot * 512, (ot + 1) * 512)
                        r.do("tensor", lambda e, ps=ps, dc=dc, n=n, wap=wap:
                             e.matmul(ps[:],
                                      t["bigA"][:, dc * 2048 + n * 128:
                                                dc * 2048 + (n + 1) * 128],
                                      wap,
                                      start=(dc == 0), stop=False))
                    r.need("tensor", s["SV"], st9["g_mark"][gp])
                    r.do("tensor", lambda e, ps=ps, gsb=gsb, ot=ot: e.matmul(
                        ps[:], gsb[:], b_sb[:, ot * 512:(ot + 1) * 512],
                        start=False, stop=True), inc=(s["SP"], 1))
                    mm = s["SP"].n
                    r.need("vector", s["SP"], mm)
                    r.need("vector", s["SXL"][par], st9["xl"][par])
                    ev = r.do("vector", lambda e, xn=xn, ps=ps, xb=xb, ot=ot:
                              e.tensor_tensor(xn[:, ot * 512:(ot + 1) * 512],
                                              ps[:],
                                              xb[:, ot * 512:(ot + 1) * 512],
                                              ALU.add), inc=(s["SV"], 1))
                    ps_free[ot] = ev
                st9["g_free"][gp] = s["SP"].n
                if n + 1 < NST:
                    gating_pe(n + 1)
                x_free[par] = s["SV"].n
                r.need("sync", s["SV"], s["SV"].n)
                stm = r.do("sync", lambda e, xn=xn, n=n, dst=dst: e.dma_start(
                    dst[n * 128:(n + 1) * 128, :], xn[:]), inc=(sxn, 16))
                xn_free[0] = stm
        return [(s["SXN"][i], s["SXN"][i].n) for i in range(2)]



def build_nc(n_sub=4, dbg=False):
    nc = bass.Bass("TRN2", target_bir_lowering=False)
    dram_in = {}

    def din(name, shape):
        dram_in[name] = nc.dram_tensor(name, list(shape), F32,
                                       kind="ExternalInput").ap()
        return dram_in[name]

    x_in = din("x_own", (NTOK, D))
    for nm_ in ("aw_q", "aw_k", "aw_v", "aw_o", "bw_q", "bw_k", "bw_v", "bw_o",
                "fw_t"):
        din(nm_, (128, DC * 2048))
    for nm_ in ("as_q", "as_k", "as_v", "as_o"):
        din(nm_, (DC, 4, 2048))
    din("ab_qk", (128, 32))
    din("ab_vrow", (1, 2048))
    din("ab_orow", (1, 2048))
    din("bb_qk", (128, 32))
    din("bb_vrow", (1, 2048))
    din("bb_orow", (1, 2048))
    din("fw_a", (128, 2048))
    din("fw_b", (128, 2048))
    din("fw_r", (128, 128))
    din("c_ident", (128, 128))
    din("c_msec0", (128, 128))
    din("c_msec1", (128, 128))
    din("c_exp8", (8, 128))

    y = nc.dram_tensor("y_own", [NTOK, D], F32, kind="ExternalOutput").ap()
    x_cur = nc.dram_tensor("x_cur", [NTOK, D], F32, kind="Internal").ap()
    q_dram = nc.dram_tensor("q_dram", [D, NTOK], BF16, kind="Internal").ap()
    kv_own = nc.dram_tensor("kv_own", [2 * NTOK, D], BF16, kind="Internal")
    kvf_K = nc.dram_tensor("kvf_K", [2 * NTOK, D], BF16, kind="Internal")
    kvf_V = nc.dram_tensor("kvf_V", [2 * NTOK, D], BF16, kind="Internal")

    rec = Rec()
    from contextlib import ExitStack
    with ExitStack() as stack:
        e = stack.enter_context
        sems = {}
        for nm_ in ("SP", "SV", "SA", "SCK", "SCV", "SWC", "SCN", "SCNW"):
            sems[nm_] = Sem(e(nc.semaphore(nm_)), nm_)
        sems["SV4"] = [Sem(e(nc.semaphore(f"SV4{i}")), f"SV4{i}")
                       for i in range(2)]
        for nm_, cnt in (("SXL", 2), ("SXN", 2), ("SSR", NSREP), ("SKQ", 2),
                         ("STG", 4), ("SWH", 3), ("SQG", 2)):
            sems[nm_] = [Sem(e(nc.semaphore(f"{nm_}{i}")), f"{nm_}{i}")
                         for i in range(cnt)]

        t = {}
        t["bigA"] = e(nc.sbuf_tensor("bigA", [128, DC * 2048], BF16))
        t["bigW"] = e(nc.sbuf_tensor("bigW", [128, DC * 2048], BF16))
        t["pbuf"] = e(nc.sbuf_tensor("pbuf", [128, 8 * 2048], BF16))
        t["srep"] = [e(nc.sbuf_tensor(f"srep{i}", [128, 2048], BF16))
                     for i in range(NSREP)]
        t["ident_bf"] = e(nc.sbuf_tensor("ident_bf", [128, 128], BF16))
        t["mask_sec"] = [e(nc.sbuf_tensor(f"msec{i}", [128, 128], BF16))
                         for i in range(2)]
        t["ones_col"] = e(nc.sbuf_tensor("ones_col", [128, 1], BF16))
        t["q_dram"] = q_dram
        t["exp8_dram"] = dram_in["c_exp8"]
        t["ident_dram"] = dram_in["c_ident"]
        t["kvf_K"] = kvf_K.ap()
        t["kvf_V"] = kvf_V.ap()

        bld = Build(nc, rec, sems, t)
        r, s = rec, sems

        r.do("gpsimd", lambda e: e.dma_start(t["ident_bf"][:],
                                             dram_in["c_ident"]),
             inc=(s["SCNW"], 16))
        r.do("gpsimd", lambda e: e.dma_start(t["mask_sec"][0][:],
                                             dram_in["c_msec0"]),
             inc=(s["SCNW"], 16))
        r.do("gpsimd", lambda e: e.dma_start(t["mask_sec"][1][:],
                                             dram_in["c_msec1"]),
             inc=(s["SCNW"], 16))
        r.do("vector", lambda e: e.memset(t["ones_col"][:], 1.0),
             inc=(s["SV"], 1))
        r.need("tensor", s["SCN"])
        r.need("tensor", s["SCNW"])
        r.need("tensor", s["SV"])
        r.need("scalar", s["SCN"])
        r.need("scalar", s["SCNW"])
        r.need("vector", s["SCN"])
        r.need("vector", s["SCNW"])

        def load_biases(pref, b_qk):
            r.need("sync", s["SP"])
            r.do("sync", lambda e, b_qk=b_qk: e.dma_start(
                b_qk[:], dram_in[pref + "b_qk"]), inc=(s["SCN"], 16))
            return s["SCN"].n

        def attn_core(pref, is_a, src):
            wp = "a" if is_a else "b"
            with ExitStack() as _as:
                _ae = _as.enter_context
                ps_p = [_ae(nc.psum_tensor(bld.nm("ps_p"), [128, 512], F32))
                        for _ in range(3)]
                b_qk = _ae(nc.sbuf_tensor(bld.nm("bqk"), [128, 32], F32))
                stage = [_ae(nc.sbuf_tensor(bld.nm("sgp"), [128, 512], BF16))
                         for _ in range(4)]
                t["b_qk"] = b_qk
                share = {"ps": [None, None, None], "stg": [0, 0, 0, 0]}
                wsel, ready, ab = bld.load_weight(
                    dram_in[wp + "w_k"], dram_in["as_k"] if is_a else None)
                bias_m = load_biases(pref, b_qk)
                gmarks = bld.ln_phase(src)
                ln_sa, ln_sv = s["SA"].n, s["SV"].n
                r.need("scalar", s["SCN"], bias_m)
                r.need("tensor", s["SCN"], bias_m)
                bld.proj_fm(ps_p, stage, gmarks, wsel, ready,
                            t["b_qk"][:, 16:32], kv_own.ap(), 0, share)
                bld.wring_release(ab)

                def k_coll():
                    for sg in s["STG"]:
                        r.need("gpsimd", sg, sg.n)
                    for ci in range(4):
                        r.do("gpsimd", lambda e, ci=ci: e.collective_compute(
                            "AllGather", ALU.bypass,
                            replica_groups=[[0, 1], [2, 3], [4, 5], [6, 7]],
                            ins=[kv_own.ap()[ci * 512:(ci + 1) * 512,
                                             :].opt()],
                            outs=[kvf_K.ap()[ci * 1024:(ci + 1) * 1024,
                                             :].opt()]),
                            inc=(s["SCK"], 1))
                wsel, ready, ab = bld.load_weight(
                    dram_in[wp + "w_v"], dram_in["as_v"] if is_a else None,
                    mid=k_coll)
                wsel_q, ready_q, ab_q, fin_q = bld.load_weight(
                    dram_in[wp + "w_q"], dram_in["as_q"] if is_a else None,
                    defer_b=True)
                bld.proj_tm(ps_p, stage, gmarks, wsel, ready,
                            dram_in[pref + "b_vrow"], kv_own.ap(), NTOK,
                            share, (ln_sa, ln_sv))
                bld.wring_release(ab)

                def v_coll():
                    for sg in s["STG"]:
                        r.need("gpsimd", sg, sg.n)
                    for ci in range(4):
                        r.do("gpsimd", lambda e, ci=ci: e.collective_compute(
                            "AllGather", ALU.bypass,
                            replica_groups=[[0, 1], [2, 3], [4, 5], [6, 7]],
                            ins=[kv_own.ap()[NTOK + ci * 512:
                                             NTOK + (ci + 1) * 512,
                                             :].opt()],
                            outs=[kvf_V.ap()[ci * 1024:(ci + 1) * 1024,
                                             :].opt()]),
                            inc=(s["SCV"], 1))
                ready_q = fin_q(v_coll)
                q_marks = bld.proj_fm(ps_p, stage, gmarks, wsel_q, ready_q,
                                      t["b_qk"][:, 0:16], q_dram, 0, share)
                bld.wring_release(ab_q)
            wo_sel, wo_ready, wo_ab = bld.load_weight(
                dram_in[wp + "w_o"], dram_in["as_o"] if is_a else None)
            halves = [(t["bigW"], 0), (t["bigW"], 8 * 2048), (t["pbuf"], 0)]
            fh = 3 - wo_ab[0] - wo_ab[1]
            ften, foff = halves[fh]
            v2b_ap = ften[:, foff:foff + 8192]
            bld.attention(not is_a, q_marks, v2b_ap)
            bld.wring_free[fh] = s["SP"].n
            return wo_sel, wo_ready, wo_ab

        def attn_sublayer(pref, is_a, src, dst):
            wo_sel, wo_ready, wo_ab = attn_core(pref, is_a, src)
            bld.barrier()
            out_marks = bld.oproj_phase(wo_sel, wo_ready,
                                        dram_in[pref + "b_orow"], src, dst)
            bld.wring_release(wo_ab)
            bld.barrier()
            return out_marks

        def ffn_sublayer(src, dst):
            with ExitStack() as _fs:
                a_sb = _fs.enter_context(
                    nc.sbuf_tensor(bld.nm("a_sb"), [128, 2048], BF16))
                b_sb = _fs.enter_context(
                    nc.sbuf_tensor(bld.nm("b_sb"), [128, 2048], BF16))
                r_sb = _fs.enter_context(
                    nc.sbuf_tensor(bld.nm("r_sb"), [128, 128], BF16))
                wsel, ready, ab = bld.load_weight(dram_in["fw_t"], None)
                r.need("gpsimd", s["SP"])
                r.do("gpsimd", lambda e, a_sb=a_sb: e.dma_start(
                    a_sb[:], dram_in["fw_a"]), inc=(s["SWC"], 16))
                r.do("gpsimd", lambda e, b_sb=b_sb: e.dma_start(
                    b_sb[:], dram_in["fw_b"]), inc=(s["SWC"], 16))
                m3 = r.do("gpsimd", lambda e, r_sb=r_sb: e.dma_start(
                    r_sb[:], dram_in["fw_r"]), inc=(s["SWC"], 16))
                gmarks = bld.ln_phase(src)
                ready2 = list(ready) + [(s["SWC"], m3)]
                out_marks = bld.ffn_phase(wsel, ready2, src, dst,
                                          (a_sb, b_sb, r_sb), gmarks)
                bld.wring_release(ab)
                bld.barrier()
            return out_marks

        final_marks = []
        if n_sub < 0:
            bld.ln_phase(x_in)
            _dbstk = ExitStack()
            if n_sub <= -2:
                with nc.psum_tensor(bld.nm("ps_p0"), [128, 512], F32) as pp0, \
                        nc.psum_tensor(bld.nm("ps_p1"), [128, 512], F32) as pp1, \
                        nc.psum_tensor(bld.nm("ps_p2"), [128, 512], F32) as pp2:
                    ps_p = [pp0, pp1, pp2]
                    b_qk = _dbstk.enter_context(
                        nc.sbuf_tensor(bld.nm("bqk"), [128, 32], F32))
                    stage = [_dbstk.enter_context(nc.sbuf_tensor(
                        bld.nm("sgp"), [128, 512], BF16)) for _ in range(4)]
                    t["b_qk"] = b_qk
                    bias_m = load_biases("a", b_qk)
                    wsel, ready, ab = bld.load_weight(dram_in["aw_k"],
                                                      dram_in["as_k"])
                    gmarks = [s["SV"].n] * 4
                    r.need("scalar", s["SCN"], bias_m)
                    r.need("tensor", s["SCN"], bias_m)
                    share = {"ps": [None, None, None], "stg": [0, 0, 0, 0]}
                    bld.proj_fm(ps_p, stage, gmarks, wsel, ready,
                                t["b_qk"][:, 16:32], kv_own.ap(), 0, share)
                    bld.wring_release(ab)
                    if n_sub <= -3:
                        wsel, ready, ab = bld.load_weight(dram_in["aw_v"],
                                                          dram_in["as_v"])
                        bld.proj_tm(ps_p, stage, gmarks, wsel, ready,
                                    dram_in["ab_vrow"], kv_own.ap(), NTOK,
                                    share)
                        bld.wring_release(ab)
            bld.barrier()

        seq = []
        if n_sub >= 1:
            seq.append(("attn", "a", x_in, x_cur if n_sub > 1 else y))
        if n_sub >= 2:
            seq.append(("ffn", None, x_cur, x_cur if n_sub > 2 else y))
        if n_sub >= 3:
            seq.append(("attn", "b", x_cur, x_cur if n_sub > 3 else y))
        if n_sub >= 4:
            seq.append(("ffn", None, x_cur, y))

        for kind, which, src, dst in seq:
            if kind == "attn":
                final_marks = attn_sublayer(which, which == "a", src, dst)
            else:
                final_marks = ffn_sublayer(src, dst)

        for sem, val in final_marks:
            r.need("sync", sem, val)

        if dbg:
            dq = nc.dram_tensor("dbg_q", [D, NTOK], BF16,
                                kind="ExternalOutput").ap()
            dkv = nc.dram_tensor("dbg_kv", [2 * NTOK, D], BF16,
                                 kind="ExternalOutput").ap()
            dkvf = nc.dram_tensor("dbg_kvf", [4 * NTOK, D], BF16,
                                  kind="ExternalOutput").ap()
            dba = nc.dram_tensor("dbg_bigA", [128, DC * 2048], BF16,
                                 kind="ExternalOutput").ap()
            dbw = nc.dram_tensor("dbg_bigW", [128, DC * 2048], BF16,
                                 kind="ExternalOutput").ap()
            bld.barrier()
            r.do("sync", lambda e: e.dma_start(dq, q_dram), inc=(s["SCN"], 16))
            r.do("sync", lambda e: e.dma_start(dkv, kv_own.ap()),
                 inc=(s["SCN"], 16))
            r.do("sync", lambda e: e.dma_start(dkvf, kv_full.ap()),
                 inc=(s["SCN"], 16))
            r.do("sync", lambda e: e.dma_start(dba, t["bigA"][:]),
                 inc=(s["SCN"], 16))
            r.do("sync", lambda e: e.dma_start(dbw, t["bigW"][:]),
                 inc=(s["SCN"], 16))
            r.need("sync", s["SCN"])

        with nc.Block() as block:
            @block.sync
            def _(eng):
                for fn in rec.q["sync"]:
                    fn(eng)

            @block.tensor
            def _(eng):
                for fn in rec.q["tensor"]:
                    fn(eng)

            @block.vector
            def _(eng):
                for fn in rec.q["vector"]:
                    fn(eng)

            @block.scalar
            def _(eng):
                for fn in rec.q["scalar"]:
                    fn(eng)

            @block.gpsimd
            def _(eng):
                for fn in rec.q["gpsimd"]:
                    fn(eng)

    return nc



def _chunk_interleave(w):
    return np.ascontiguousarray(
        w.reshape(DC, 128, 2048).transpose(1, 0, 2).reshape(128, DC * 2048))


def _bias_pack(b):
    return np.ascontiguousarray(np.asarray(b, np.float32).reshape(DC, 128).T)


def make_core_inputs(core, I):
    b, p = core // 2, core % 2
    rows = np.arange(NTOK)
    gidx = (2 * (rows // 128) + p) * 128 + rows % 128
    x = np.ascontiguousarray(np.asarray(I["L0_attn_a_x"], np.float32)[b][gidx])

    d = {"x_own": x}
    for nm_, key in (("q", "wq"), ("k", "wk"), ("v", "wv"), ("o", "wo")):
        d[f"aw_{nm_}"] = _chunk_interleave(
            np.asarray(I[f"L0_attn_a_{key}_codes"], np.float32))
        d[f"as_{nm_}"] = np.ascontiguousarray(
            np.asarray(I[f"L0_attn_a_{key}_scales"], np.float32)
            .reshape(DC, 4, 2048))
    d["ab_qk"] = np.concatenate(
        [_bias_pack(I["L0_attn_a_bq"]), _bias_pack(I["L0_attn_a_bk"])], axis=1)
    d["ab_vrow"] = np.asarray(I["L0_attn_a_bv"], np.float32).reshape(1, 2048)
    d["ab_orow"] = np.asarray(I["L0_attn_a_bo"], np.float32).reshape(1, 2048)
    for nm_, key in (("q", "Wq"), ("k", "Wk"), ("v", "Wv")):
        w = np.asarray(I[f"L0_attn_b_{key}"], np.float32).transpose(1, 0, 2)
        d[f"bw_{nm_}"] = _chunk_interleave(
            np.ascontiguousarray(w.reshape(D, 2048)))
    d["bw_o"] = _chunk_interleave(np.asarray(I["L0_attn_b_Wo"], np.float32))
    d["bb_qk"] = np.concatenate(
        [_bias_pack(np.asarray(I["L0_attn_b_bq"], np.float32).reshape(-1)),
         _bias_pack(np.asarray(I["L0_attn_b_bk"], np.float32).reshape(-1))],
        axis=1)
    d["bb_vrow"] = np.asarray(I["L0_attn_b_bv"], np.float32).reshape(1, 2048)
    d["bb_orow"] = np.asarray(I["L0_attn_b_bo"], np.float32).reshape(1, 2048)
    d["fw_t"] = _chunk_interleave(
        np.ascontiguousarray(np.asarray(I["L0_ffn_weight"], np.float32).T))
    d["fw_a"] = np.ascontiguousarray(
        np.asarray(I["L0_ffn_lora_A"], np.float32).transpose(2, 0, 1)
        .reshape(D, 128).reshape(DC, 128, 128).transpose(1, 0, 2)
        .reshape(128, 2048))
    d["fw_b"] = np.ascontiguousarray(
        np.asarray(I["L0_ffn_lora_B"], np.float32).transpose(0, 2, 1)
        .reshape(128, D))
    d["fw_r"] = np.ascontiguousarray(
        np.asarray(I["L0_ffn_router_w"], np.float32).T.reshape(DC, 128, 8)
        .transpose(1, 0, 2).reshape(128, 128))
    d["c_ident"] = np.eye(128, dtype=np.float32)
    tri = np.triu(np.ones((128, 128), np.float32))
    bound = (np.ones if p == 1 else np.zeros)((128, 128), np.float32)
    d["c_msec0"] = tri if p == 0 else bound
    d["c_msec1"] = bound if p == 0 else tri
    ex = np.zeros((8, 128), np.float32)
    for e_ in range(8):
        ex[e_, e_ * 16:(e_ + 1) * 16] = 2.0
    d["c_exp8"] = ex
    return d


_NC_CACHE = {}


def kernel(**inputs):
    n_sub = int(inputs.pop("_n_sub", 4))
    dbg = bool(inputs.pop("_dbg", False))
    key = (n_sub, dbg)
    if key not in _NC_CACHE:
        _NC_CACHE[key] = build_nc(n_sub, dbg)
    nc = _NC_CACHE[key]
    in_maps = [make_core_inputs(c, inputs) for c in range(NCORES)]
    res = run_bass_kernel_spmd(nc, in_maps, core_ids=list(range(NCORES)))
    if dbg:
        kernel._dbg_res = res
    out = np.zeros((B, S, D), np.float32)
    for c in range(NCORES):
        b, p = c // 2, c % 2
        rows = np.arange(NTOK)
        gidx = (2 * (rows // 128) + p) * 128 + rows % 128
        out[b][gidx] = np.asarray(res.results[c]["y_own"], np.float32)
    return out


if __name__ == "__main__":
    nc = build_nc(int(sys.argv[1]) if len(sys.argv) > 1 else 4)
    print("built ok")


# version 19
# speedup vs baseline: 1.0372x; 1.0372x over previous
import sys
import numpy as np

sys.path.insert(0, "/opt/trn_rl_repo")

import concourse.bass as bass
import concourse.mybir as mybir
from concourse.bass_utils import run_bass_kernel_spmd

F32 = mybir.dt.float32
BF16 = mybir.dt.bfloat16
AF = mybir.ActivationFunctionType
ALU = mybir.AluOpType
AX = mybir.AxisListType.X

NCORES = 8
B, S, D = 4, 4096, 2048
H, DH = 16, 128
NTOK = 2048
NST = 16
DC = 16
NSREP = 2
EPS = 1e-5
ISQDH = float(1.0 / np.sqrt(DH))

ENGINES = ("sync", "tensor", "vector", "scalar", "gpsimd")


class Sem:
    __slots__ = ("h", "n", "name")

    def __init__(self, h, name):
        self.h = h
        self.n = 0
        self.name = name


class Rec:
    def __init__(self):
        self.q = {e: [] for e in ENGINES}
        self.waited = {e: {} for e in ENGINES}
        self.ninstr = 0

    def do(self, eng, make, inc=None):
        if inc is not None:
            sem, amt = inc
            sem.n += amt
            h, a = sem.h, amt

            def fn(e, make=make, h=h, a=a):
                make(e).then_inc(h, a)
        else:
            def fn(e, make=make):
                make(e)
        self.q[eng].append(fn)
        self.ninstr += 1
        return inc[0].n if inc is not None else None

    def need(self, eng, sem, val=None):
        if val is None:
            val = sem.n
        if val <= 0:
            return
        w = self.waited[eng]
        if w.get(sem.name, -1) >= val:
            return
        w[sem.name] = val
        self.q[eng].append(lambda e, h=sem.h, v=val: e.wait_ge(h, v))
        self.ninstr += 1


class Build:
    def __init__(self, nc, rec, sems, t):
        self.nc = nc
        self.r = rec
        self.s = sems
        self.t = t
        self.uid = 0
        self.all_sems = [v for v in sems.values() if isinstance(v, Sem)]
        for v in sems.values():
            if isinstance(v, list):
                self.all_sems.extend(v)

    def nm(self, base):
        self.uid += 1
        return f"{base}_{self.uid}"

    def barrier(self):
        for e in ENGINES:
            if e == "gpsimd":
                continue
            for s_ in self.all_sems:
                self.r.need(e, s_, s_.n)

    def entry_wait(self, eng="tensor"):
        for e in (eng, "sync"):
            self.r.need(e, self.s["SV"])
            self.r.need(e, self.s["SA"])
            self.r.need(e, self.s["SP"])

    def ln_phase(self, src):
        r, s, t = self.r, self.s, self.t
        nc = self.nc
        from contextlib import ExitStack
        with ExitStack() as _stk:
            _e = _stk.enter_context
            ps_t = [_e(nc.psum_tensor(self.nm("ps_t"), [128, 512], BF16))
                    for _ in range(2)]
            xtile = [_e(nc.sbuf_tensor(self.nm("xt"), [128, 2048], F32))
                     for _ in range(2)]
            xln_tm = [_e(nc.sbuf_tensor(self.nm("xm"), [128, 2048], BF16))
                      for _ in range(2)]
            stats = [_e(nc.sbuf_tensor(self.nm("sst"), [128, 24], F32))
                     for _ in range(2)]
            mvs = [_e(nc.sbuf_tensor(self.nm("mvv"), [128, 2], F32))
                   for _ in range(2)]
            lnts = [_e(nc.sbuf_tensor(self.nm("lt"), [128, 3], F32))
                    for _ in range(2)]
            self.entry_wait("tensor")
            gmarks = []
            apply_done = [0, 0]
            tp_done = [0, 0]
            evac_done = [0, 0]
            for st in range(NST):
                par = st % 2
                xb = xtile[par]
                sxl = s["SXL"][par]
                r.need("sync", s["SA"], apply_done[par])
                ld = r.do("sync", lambda e, xb=xb, st=st, src=src: e.dma_start(
                    xb[:], src[st * 128:(st + 1) * 128, :]), inc=(sxl, 16))
                r.need("vector", sxl, ld)
                r.need("vector", s["SA"], apply_done[par])
                stt = stats[par]
                for i in range(4):
                    inc = (s["SV"], 1) if i == 3 else None
                    r.do("vector", lambda e, i=i, xb=xb, stt=stt: e.bn_stats(
                        stt[:, 6 * i:6 * i + 6], xb[:, 512 * i:512 * (i + 1)]),
                        inc=inc)
                r.need("vector", s["SV"])
                mv = mvs[par]
                r.do("vector", lambda e, mv=mv, stt=stt: e.bn_aggr(mv[:], stt[:]),
                     inc=(s["SV"], 1))
                r.need("vector", s["SV"])
                lnt = lnts[par]
                r.do("vector", lambda e, lnt=lnt, mv=mv: e.tensor_scalar_add(
                    lnt[:, 0:1], mv[:, 1:2], EPS), inc=(s["SV"], 1))
                r.need("scalar", s["SV"])
                r.do("scalar", lambda e, lnt=lnt: e.activation(
                    lnt[:, 0:1], lnt[:, 0:1], AF.Ln), inc=(s["SA"], 1))
                r.need("scalar", s["SA"])
                r.do("scalar", lambda e, lnt=lnt: e.activation(
                    lnt[:, 1:2], lnt[:, 0:1], AF.Exp, scale=-0.5),
                    inc=(s["SA"], 1))
                r.need("vector", s["SA"])
                r.do("vector", lambda e, lnt=lnt, mv=mv: e.scalar_tensor_tensor(
                    lnt[:, 2:3], mv[:, 0:1], -1.0, lnt[:, 1:2],
                    ALU.mult, ALU.mult), inc=(s["SV"], 1))
                xm = xln_tm[par]
                r.need("scalar", s["SV"])
                r.need("scalar", s["SP"], tp_done[par])
                av = r.do("scalar", lambda e, xm=xm, xb=xb, lnt=lnt: e.activation(
                    xm[:], xb[:], AF.Identity, bias=lnt[:, 2:3],
                    scale=lnt[:, 1:2]), inc=(s["SA"], 1))
                apply_done[par] = av
                r.need("tensor", s["SA"], av)
                for g4 in range(4):
                    p2 = g4 % 2
                    pst = ps_t[p2]
                    r.need("tensor", s["SV"], evac_done[p2])
                    for k in range(4):
                        dc = g4 * 4 + k
                        inc = (s["SP"], 1) if k == 3 else None
                        r.do("tensor", lambda e, pst=pst, k=k, xm=xm, dc=dc:
                             e.transpose(pst[:, k * 128:(k + 1) * 128],
                                         xm[:, dc * 128:(dc + 1) * 128],
                                         t["ident_bf"][:]), inc=inc)
                    tpv = s["SP"].n
                    if g4 == 3:
                        tp_done[par] = tpv
                    r.need("vector", s["SP"], tpv)
                    dst = t["bigA"][:].rearrange("p (dc n) -> p dc n", n=NTOK)[
                        :, g4 * 4:(g4 + 1) * 4, st * 128:(st + 1) * 128]
                    srcap = pst[:].rearrange("p (k n) -> p k n", n=128)
                    ev = r.do("vector", lambda e, dst=dst, srcap=srcap:
                              e.tensor_copy(dst, srcap), inc=(s["SV"], 1))
                    evac_done[p2] = ev
                if st % 4 == 3:
                    gmarks.append(s["SV"].n)
        return gmarks

    def load_weight(self, codes, scales, mid=None, defer_b=False):
        r, s, t = self.r, self.s, self.t
        if not hasattr(self, "wring_free"):
            self.wring_free = [0, 0, 0]
            self.wring_a = 0
            self.srep_free = [0] * NSREP
        a = self.wring_a
        b = (a + 1) % 3
        self.wring_a = (a + 2) % 3
        halves = [(t["bigW"], 0), (t["bigW"], 8 * 2048), (t["pbuf"], 0)]

        def wsel(dc, c0, c1, a=a, b=b, halves=halves):
            h = a if dc < 8 else b
            ten, off = halves[h]
            base = off + (dc % 8) * 2048
            return ten[:, base + c0: base + c1]

        ready = []

        def emit_half(h, src_col, dclo):
            r.need("gpsimd", s["SP"], self.wring_free[h])
            ten, off = halves[h]
            m = 0
            for part in range(4):
                m = r.do("gpsimd", lambda e, ten=ten, off=off, part=part,
                         src_col=src_col, codes=codes: e.dma_start(
                             ten[part * 32:(part + 1) * 32,
                                 off: off + 8 * 2048],
                             codes[part * 32:(part + 1) * 32,
                                   src_col: src_col + 8 * 2048]),
                         inc=(s["SWH"][h], 16))
            ready.append((s["SWH"][h], m))
            if scales is not None:
                for dc in range(dclo, dclo + 8):
                    par = dc % NSREP
                    sr = t["srep"][par]
                    ssr = s["SSR"][par]
                    r.need("gpsimd", s["SV"], self.srep_free[par])
                    srcap = scales[dc:dc + 1, :, :].rearrange(
                        "dc g f -> g dc f")
                    srcap = srcap.unsqueeze(1).broadcast_to((4, 32, 1, 2048))
                    sm = 0
                    for ai in range(4):
                        sm = r.do("gpsimd", lambda e, sr=sr, srcap=srcap,
                                  ai=ai: e.dma_start(
                                      sr[ai * 32:(ai + 1) * 32, :],
                                      srcap[ai:ai + 1]), inc=(ssr, 16))
                    r.need("vector", ssr, sm)
                    r.need("vector", s["SWH"][h], m)
                    dst = wsel(dc, 0, 2048)
                    ev = r.do("vector", lambda e, dst=dst, sr=sr:
                              e.tensor_tensor(dst, dst, sr[:], ALU.mult),
                              inc=(s["SV"], 1))
                    self.srep_free[par] = ev

        emit_half(a, 0, 0)

        def finish(mid2=None):
            if mid2 is not None:
                mid2()
            emit_half(b, 8 * 2048, 8)
            if scales is not None:
                ready.append((s["SV"], s["SV"].n))
            return ready

        if defer_b:
            return wsel, ready, (a, b), finish
        finish(mid)
        return wsel, ready, (a, b)

    def wring_release(self, ab):
        a, b = ab
        self.wring_free[a] = self.s["SP"].n
        self.wring_free[b] = self.s["SP"].n

    def proj_fm(self, ps_p, stage, gmarks, wsel, ready, bcol_ap, dst_dram,
                dst_row0, share):
        r, s, t = self.r, self.s, self.t
        psfree = share["ps"]
        stfree = share["stg"]
        k = 0
        for sem, val in ready:
            r.need("tensor", sem, val)
        for g in range(4):
            r.need("tensor", s["SV"], gmarks[g])
            for f in range(DC):
                ps = ps_p[k % 3]
                if psfree[k % 3] is not None:
                    r.need("tensor", psfree[k % 3][0], psfree[k % 3][1])
                for dc in range(DC):
                    inc = (s["SP"], 1) if dc == DC - 1 else None
                    wap = wsel(dc, f * 128, (f + 1) * 128)
                    r.do("tensor", lambda e, ps=ps, dc=dc, g=g, wap=wap:
                         e.matmul(
                             ps[:], wap,
                             t["bigA"][:, dc * 2048 + g * 512:
                                       dc * 2048 + (g + 1) * 512],
                             start=(dc == 0), stop=(dc == DC - 1)),
                         inc=inc)
                mm = s["SP"].n
                slot = k % 4
                stg = stage[slot]
                sst = s["STG"][slot]
                r.need("scalar", s["SP"], mm)
                r.need("scalar", sst, stfree[slot])
                ev = r.do("scalar", lambda e, stg=stg, ps=ps, f=f:
                          e.activation(stg[:], ps[:], AF.Identity,
                                       bias=bcol_ap[:, f:f + 1]),
                          inc=(s["SA"], 1))
                psfree[k % 3] = (s["SA"], ev)
                r.need("sync", s["SA"], ev)
                st_ = r.do("sync", lambda e, stg=stg, dst_dram=dst_dram,
                           f=f, g=g: e.dma_start(dst_dram[
                               dst_row0 + f * 128: dst_row0 + (f + 1) * 128,
                               g * 512:(g + 1) * 512], stg[:]),
                           inc=(sst, 16))
                stfree[slot] = st_
                k += 1
        return [(s["STG"][i], s["STG"][i].n) for i in range(4)]

    def proj_tm(self, ps_p, stage, gmarks, wsel, ready, brow_dram, dst_dram,
                dst_row0, share, ln_marks=None):
        r, s, t = self.r, self.s, self.t
        psfree = share["ps"]
        stfree = share["stg"]
        k = 0
        for sem, val in ready:
            r.need("tensor", sem, val)
        from contextlib import ExitStack
        with ExitStack() as _stk:
            brow = _stk.enter_context(self.nc.sbuf_tensor(
                self.nm("brow"), [1, 2048], BF16))
            ones_row = _stk.enter_context(self.nc.sbuf_tensor(
                self.nm("onr"), [1, 128], BF16))
            brow_ap = brow[:]
            if ln_marks is not None:
                r.need("gpsimd", s["SA"], ln_marks[0])
                r.need("gpsimd", s["SV"], ln_marks[1])
            else:
                r.need("gpsimd", s["SP"])
            bm = r.do("gpsimd", lambda e, brow=brow, brow_dram=brow_dram:
                      e.dma_start(brow[:], brow_dram), inc=(s["SCNW"], 16))
            r.need("tensor", s["SCNW"], bm)
            r.need("vector", s["SA"])
            mv = r.do("vector", lambda e, ones_row=ones_row: e.memset(
                ones_row[:], 1.0), inc=(s["SV"], 1))
            r.need("tensor", s["SV"], mv)
            for n in range(DC):
                r.need("tensor", s["SV"], gmarks[n // 4])
                for ft in range(4):
                    ps = ps_p[k % 3]
                    if psfree[k % 3] is not None:
                        r.need("tensor", psfree[k % 3][0], psfree[k % 3][1])
                    for dc in range(DC):
                        wap = wsel(dc, ft * 512, (ft + 1) * 512)
                        r.do("tensor", lambda e, ps=ps, dc=dc, n=n, wap=wap:
                             e.matmul(ps[:],
                                      t["bigA"][:, dc * 2048 + n * 128:
                                                dc * 2048 + (n + 1) * 128],
                                      wap,
                                      start=(dc == 0), stop=False))
                    r.do("tensor", lambda e, ps=ps, ft=ft, brow_ap=brow_ap,
                         ones_row=ones_row:
                         e.matmul(ps[:], ones_row[:],
                                  brow_ap[:, ft * 512:(ft + 1) * 512],
                                  start=False, stop=True), inc=(s["SP"], 1))
                    mm = s["SP"].n
                    slot = k % 4
                    stg = stage[slot]
                    sst = s["STG"][slot]
                    r.need("vector", s["SP"], mm)
                    r.need("vector", sst, stfree[slot])
                    ev = r.do("vector", lambda e, stg=stg, ps=ps: e.tensor_copy(
                        stg[:], ps[:]), inc=(s["SV"], 1))
                    psfree[k % 3] = (s["SV"], ev)
                    r.need("sync", s["SV"], ev)
                    st_ = r.do("sync", lambda e, stg=stg, dst_dram=dst_dram,
                               n=n, ft=ft: e.dma_start(dst_dram[
                                   dst_row0 + n * 128: dst_row0 + (n + 1) * 128,
                                   ft * 512:(ft + 1) * 512], stg[:]),
                               inc=(sst, 16))
                    stfree[slot] = st_
                    k += 1
        return [(s["STG"][i], s["STG"][i].n) for i in range(4)]

    def attention(self, causal, q_marks, v2b_ap):
        r, s, t = self.r, self.s, self.t
        self.entry_wait("tensor")
        for sem, val in q_marks:
            r.need("sync", sem, val)
            r.need("scalar", sem, val)
        nc = self.nc
        from contextlib import ExitStack
        with ExitStack() as _stk:
            _e = _stk.enter_context
            ps_S = [_e(nc.psum_tensor(self.nm("ps_S"), [128, 1024], F32))
                    for _ in range(2)]
            ps_O = _e(nc.psum_tensor(self.nm("ps_O"), [128, 512], F32))
            ps_sums = _e(nc.psum_tensor(self.nm("ps_su"), [128, 4], F32))
            ps_T = _e(nc.psum_tensor(self.nm("ps_T"), [128, 512], BF16))
            V4sb = _e(nc.sbuf_tensor(self.nm("V2"), [128, 32 * 256], BF16))
            K_h = [_e(nc.sbuf_tensor(self.nm("kh"), [128, 4096], BF16))
                   for _ in range(2)]
            Qg = [_e(nc.sbuf_tensor(self.nm("qg"), [128, 512], BF16))
                  for _ in range(2)]
            ptil = [_e(nc.sbuf_tensor(self.nm("pt"), [128, 1024], BF16))
                    for _ in range(2)]
            Otm = _e(nc.sbuf_tensor(self.nm("Otm"), [128, 512], BF16))
            mtile = [Otm]
            recip = _e(nc.sbuf_tensor(self.nm("rcp"), [128, 4], F32))

            v2bufs = [V4sb[:], v2b_ap]
            sp0 = s["SP"].n
            st8 = {"kq_free": [0, 0], "v4_free": [sp0, sp0], "pend": None,
                   "pv_mark": [0, 0], "swar": [0, 0], "evT": 0,
                   "mt_free": [0, 0], "mt_par": 0, "v4m": [0, 0],
                   "v2cur": None, "qg_free": [0, 0]}

            def pv_block(jlist, reg, qg, hh, expm, epoch_first, final=False):
                vb, vpar_ = st8["v2cur"]
                r.need("tensor", s["SV4"][vpar_], st8["v4m"][vpar_])
                pairs = []
                for k, j in enumerate(jlist):
                    ls = j % 16
                    for qs in range(4):
                        if causal and ls > 4 * qg + qs:
                            continue
                        pairs.append((k, j, qs))
                for pi, (k, j, qs) in enumerate(pairs):
                    ls, sec = j % 16, j // 16
                    is_last = final and (pi == len(pairs) - 1)
                    if True:
                        stq = 4 * qg + qs
                        masked = causal and (ls == stq)
                        lhs_t = None
                        if masked:
                            mp = 0
                            mt = mtile[mp]
                            r.need("vector", s["SA"], expm)
                            r.need("vector", s["SP"], st8["mt_free"][mp])
                            r.do("vector", lambda e, mt=mt, reg=reg, k=k, qs=qs,
                                 sec=sec: e.tensor_tensor(
                                     mt[:, 0:128],
                                     ptil[reg][:, k * 512 + qs * 128:
                                               k * 512 + (qs + 1) * 128],
                                     t["mask_sec"][sec][:], ALU.mult),
                                 inc=(s["SV"], 1))
                            r.need("tensor", s["SV"], s["SV"].n)
                            lhs_t = (mt, mp)
                        r.need("tensor", s["SA"], expm)
                        stO = epoch_first[0]
                        epoch_first[0] = False

                        def lhs(e, lhs_t=lhs_t, reg=reg, k=k, qs=qs):
                            if lhs_t is not None:
                                return lhs_t[0][:, 0:128]
                            return ptil[reg][:, k * 512 + qs * 128:
                                             k * 512 + (qs + 1) * 128]

                        r.do("tensor", lambda e, lhs=lhs, j=j, qs=qs, hh=hh,
                             vb=vb, stO=stO, is_last=is_last: e.matmul(
                                 ps_O[:, qs * 128:(qs + 1) * 128], lhs(e),
                                 vb[:, j * 256 + hh * 128:
                                    j * 256 + (hh + 1) * 128],
                                 start=stO, stop=is_last))
                        stSu = epoch_first[1]
                        epoch_first[1] = False
                        mm = r.do("tensor", lambda e, lhs=lhs, qs=qs, stSu=stSu,
                                  is_last=is_last:
                                  e.matmul(ps_sums[:, qs:qs + 1], lhs(e),
                                           t["ones_col"][:], start=stSu,
                                           stop=is_last), inc=(s["SP"], 1))
                        if lhs_t is not None:
                            st8["mt_free"][lhs_t[1]] = mm

            for u in range(8):
                vpar = u % 2
                vbuf = v2bufs[vpar]
                st8["v2cur"] = (vbuf, vpar)
                r.need("scalar", s["SP"], st8["v4_free"][vpar])
                if u == 0:
                    r.need("scalar", s["SCV"])
                    r.need("sync", s["SCK"])
                for sec in range(2):
                    for ci in range(4):
                        row0 = ci * 1024 + sec * 512
                        vsrc = t["kvf_V"][row0:row0 + 512,
                                          u * 256:(u + 1) * 256]
                        vsrc = vsrc.rearrange("(ls p) f -> p ls f", p=128)
                        c0 = (sec * 16 + ci * 4) * 256
                        vdst = vbuf[:, c0:c0 + 4 * 256]
                        vdst = vdst.rearrange("p (ls f) -> p ls f", f=256)
                        st8["v4m"][vpar] = r.do(
                            "scalar", lambda e, vdst=vdst, vsrc=vsrc:
                            e.dma_start(vdst, vsrc), inc=(s["SV4"][vpar], 16))
                for hh in range(2):
                    h = 2 * u + hh
                    par = hh % 2
                    kh = K_h[par]
                    skq = s["SKQ"][par]
                    r.need("sync", s["SP"], st8["kq_free"][par])
                    qm_ = 0
                    ci_, hp_ = h // 4, h % 4
                    for sec in range(2):
                        row0 = ci_ * 1024 + sec * 512 + hp_ * 128
                        qm_ = r.do("sync", lambda e, kh=kh, sec=sec, row0=row0:
                                   e.dma_start(
                                       kh[:, sec * 2048:(sec + 1) * 2048],
                                       t["kvf_K"][row0:row0 + 128, :]),
                                   inc=(skq, 16))
                    r.need("tensor", skq, qm_)
                    for qg in range(4):
                        qsl = qg % 2
                        r.need("sync", s["SP"], st8["qg_free"][qsl])
                        qgm = r.do("sync", lambda e, qsl=qsl, h=h, qg=qg:
                                   e.dma_start(
                                       Qg[qsl][:],
                                       t["q_dram"][h * 128:(h + 1) * 128,
                                                   qg * 512:(qg + 1) * 512]),
                                   inc=(s["SQG"][qsl], 16))
                        r.need("tensor", s["SQG"][qsl], qgm)
                        epoch_first = [True, True]
                        r.need("tensor", s["SV"], s["SV"].n)
                        if causal:
                            included = [j for j in range(32)
                                        if (j % 16) <= 4 * qg + 3]
                        else:
                            included = list(range(32))
                        regions = [included[i:i + 2]
                                   for i in range(0, len(included), 2)]
                        pend_pv = None
                        for ri, jgrp in enumerate(regions):
                            reg = ri % 2
                            r.need("tensor", s["SA"], st8["swar"][reg])
                            psS = ps_S[reg]
                            qga = Qg[qsl]
                            for slot, j in enumerate(jgrp):
                                r.do("tensor", lambda e, psS=psS, slot=slot,
                                     kh=kh, qga=qga, j=j: e.matmul(
                                         psS[:, slot * 512:(slot + 1) * 512],
                                         kh[:, j * 128:(j + 1) * 128],
                                         qga[:],
                                         start=True, stop=True),
                                     inc=(s["SP"], 1))
                            nsl = len(jgrp)
                            r.need("scalar", s["SP"], s["SP"].n)
                            r.need("scalar", s["SP"], st8["pv_mark"][reg])
                            pt = ptil[reg]
                            em = r.do("scalar", lambda e, psS=psS, pt=pt,
                                      nsl=nsl: e.activation(
                                          pt[:, 0:nsl * 512],
                                          psS[:, 0:nsl * 512],
                                          AF.Exp, scale=ISQDH),
                                      inc=(s["SA"], 1))
                            st8["swar"][reg] = em
                            if st8["pend"] is not None:
                                st8["pend"]()
                                st8["pend"] = None
                            if pend_pv is not None:
                                pv_block(*pend_pv)
                                st8["pv_mark"][pend_pv[1]] = s["SP"].n
                            pend_pv = (jgrp, reg, qg, hh, em, epoch_first)
                        if pend_pv is not None:
                            pv_block(*pend_pv, final=True)
                            st8["pv_mark"][pend_pv[1]] = s["SP"].n
                            pend_pv = None
                        st8["qg_free"][qsl] = s["SP"].n
                        r.need("vector", s["SP"], s["SP"].n)
                        r.do("vector", lambda e: e.reciprocal(
                            recip[:], ps_sums[:, 0:4]), inc=(s["SV"], 1))
                        r.need("vector", s["SV"])
                        for qs in range(4):
                            inc = (s["SV"], 1) if qs == 3 else None
                            r.do("vector", lambda e, qs=qs: e.tensor_scalar(
                                Otm[:, qs * 128:(qs + 1) * 128],
                                ps_O[:, qs * 128:(qs + 1) * 128],
                                recip[:, qs:qs + 1], None, ALU.mult), inc=inc)
                        nd = s["SV"].n

                        def flushT(h=h, qg=qg, nd=nd):
                            r.need("tensor", s["SV"], nd)
                            r.need("tensor", s["SV"], st8["evT"])
                            for qs in range(4):
                                inc = (s["SP"], 1) if qs == 3 else None
                                r.do("tensor", lambda e, qs=qs: e.transpose(
                                    ps_T[:, qs * 128:(qs + 1) * 128],
                                    Otm[:, qs * 128:(qs + 1) * 128],
                                    t["ident_bf"][:]), inc=inc)
                            tm = s["SP"].n
                            r.need("vector", s["SP"], tm)
                            st8["evT"] = r.do(
                                "vector", lambda e, h=h, qg=qg: e.tensor_copy(
                                    t["bigA"][:, h * 2048 + qg * 512:
                                              h * 2048 + (qg + 1) * 512],
                                    ps_T[:]), inc=(s["SV"], 1))

                        st8["pend"] = flushT
                    st8["kq_free"][par] = s["SP"].n
                st8["v4_free"][vpar] = s["SP"].n
            if st8["pend"] is not None:
                st8["pend"]()
                st8["pend"] = None

    def oproj_phase(self, wsel, ready, brow_dram, src, dst):
        r, s, t = self.r, self.s, self.t
        self.entry_wait("tensor")
        for sem, val in ready:
            r.need("tensor", sem, val)
        nc = self.nc
        from contextlib import ExitStack
        with ExitStack() as _stk:
            _e = _stk.enter_context
            ps_p = [_e(nc.psum_tensor(self.nm("ps_o"), [128, 512], F32))
                    for _ in range(2)]
            brow = _e(nc.sbuf_tensor(self.nm("brow"), [1, 2048], BF16))
            ones_row = _e(nc.sbuf_tensor(self.nm("onr"), [1, 128], BF16))
            brow_ap = brow[:]
            r.need("gpsimd", s["SP"])
            bm = r.do("gpsimd", lambda e, brow=brow, brow_dram=brow_dram:
                      e.dma_start(brow[:], brow_dram), inc=(s["SCNW"], 16))
            r.need("tensor", s["SCNW"], bm)
            r.need("vector", s["SA"])
            r.do("vector", lambda e, ones_row=ones_row: e.memset(
                ones_row[:], 1.0), inc=(s["SV"], 1))
            r.need("tensor", s["SV"])
            xtile = [_e(nc.sbuf_tensor(self.nm("oxt"), [128, 2048], F32))
                     for _ in range(2)]
            xnew = [_e(nc.sbuf_tensor(self.nm("oxn"), [128, 2048], F32))
                    for _ in range(2)]
            ps_free = [0, 0]
            x_free = [0, 0]
            xn_free = [0, 0]
            for st in range(NST):
                par = st % 2
                xb = xtile[par]
                sxl = s["SXL"][par]
                r.need("sync", s["SV"], x_free[par])
                xl = r.do("sync", lambda e, xb=xb, st=st, src=src: e.dma_start(
                    xb[:], src[st * 128:(st + 1) * 128, :]), inc=(sxl, 16))
                xn = xnew[par]
                sxn = s["SXN"][par]
                r.need("vector", sxn, xn_free[par])
                for dt_ in range(4):
                    ps = ps_p[dt_ % 2]
                    r.need("tensor", s["SV"], ps_free[dt_ % 2])
                    for f in range(DC):
                        wap = wsel(f, dt_ * 512, (dt_ + 1) * 512)
                        r.do("tensor", lambda e, ps=ps, f=f, st=st, wap=wap:
                             e.matmul(ps[:],
                                      t["bigA"][:, f * 2048 + st * 128:
                                                f * 2048 + (st + 1) * 128],
                                      wap,
                                      start=(f == 0), stop=False))
                    r.do("tensor", lambda e, ps=ps, dt_=dt_, brow_ap=brow_ap,
                         ones_row=ones_row:
                         e.matmul(ps[:], ones_row[:],
                                  brow_ap[:, dt_ * 512:(dt_ + 1) * 512],
                                  start=False, stop=True), inc=(s["SP"], 1))
                    mm = s["SP"].n
                    r.need("vector", s["SP"], mm)
                    r.need("vector", sxl, xl)
                    ev = r.do("vector", lambda e, xn=xn, ps=ps, xb=xb, dt_=dt_:
                              e.tensor_tensor(xn[:, dt_ * 512:(dt_ + 1) * 512],
                                              ps[:],
                                              xb[:, dt_ * 512:(dt_ + 1) * 512],
                                              ALU.add), inc=(s["SV"], 1))
                    ps_free[dt_ % 2] = ev
                x_free[par] = s["SV"].n
                r.need("sync", s["SV"], s["SV"].n)
                stm = r.do("sync", lambda e, xn=xn, st=st, dst=dst: e.dma_start(
                    dst[st * 128:(st + 1) * 128, :], xn[:]), inc=(sxn, 16))
                xn_free[par] = stm
        return [(s["SXN"][i], s["SXN"][i].n) for i in range(2)]

    def ffn_phase(self, wsel, ready, src, dst, absr, gmarks):
        r, s, t = self.r, self.s, self.t
        self.entry_wait("tensor")
        for sem, val in ready:
            r.need("tensor", sem, val)
        a_sb, b_sb, r_sb = absr
        nc = self.nc
        from contextlib import ExitStack
        with ExitStack() as _stk:
            _e = _stk.enter_context
            ps_r = _e(nc.psum_tensor(self.nm("ps_r"), [128, 8], F32))
            ps_wdT = _e(nc.psum_tensor(self.nm("ps_wdT"), [8, 128], BF16))
            ps_wrep = _e(nc.psum_tensor(self.nm("ps_wrp"), [128, 128], F32))
            ps_ax = _e(nc.psum_tensor(self.nm("ps_ax"), [128, 128], F32))
            ps_f = [_e(nc.psum_tensor(self.nm("ps_f"), [128, 512], F32))
                    for _ in range(4)]
            xtile = [_e(nc.sbuf_tensor(self.nm("fxt"), [128, 2048], F32))
                     for _ in range(2)]
            xnew = [_e(nc.sbuf_tensor(self.nm("fxn"), [128, 2048], F32))]
            top = _e(nc.sbuf_tensor(self.nm("top"), [128, 8], F32))
            t1s = _e(nc.sbuf_tensor(self.nm("t1s"), [128, 8], F32))
            t2s = _e(nc.sbuf_tensor(self.nm("t2s"), [128, 8], F32))
            mskd = _e(nc.sbuf_tensor(self.nm("mskd"), [128, 8], F32))
            td1 = _e(nc.sbuf_tensor(self.nm("td1"), [128, 8], F32))
            wdense = _e(nc.sbuf_tensor(self.nm("wd"), [128, 8], BF16))
            wdT_sb = _e(nc.sbuf_tensor(self.nm("wdT"), [8, 128], BF16))
            wrep_sb = _e(nc.sbuf_tensor(self.nm("wrp"), [128, 128], BF16))
            g_sb = [_e(nc.sbuf_tensor(self.nm("gsb"), [128, 128], BF16))
                    for _ in range(2)]
            exp8 = _e(nc.sbuf_tensor(self.nm("exp8"), [8, 128], BF16))
            r.need("gpsimd", s["SP"])
            e8m = r.do("gpsimd", lambda e, exp8=exp8: e.dma_start(
                exp8[:], t["exp8_dram"]), inc=(s["SCNW"], 16))
            r.need("tensor", s["SCNW"], e8m)
            ps_free = [0, 0, 0, 0]
            x_free = [0, 0]
            xn_free = [0]
            st9 = {"g_free": [0, 0], "r_read": 0, "wdt_free": 0,
                   "wrep_free": 0, "ax_free": 0, "wdense_free": 0,
                   "g_mark": [0, 0], "xl": [0, 0]}

            def x_load(n):
                par = n % 2
                xb = xtile[par]
                sxl = s["SXL"][par]
                r.need("sync", s["SV"], x_free[par])
                st9["xl"][par] = r.do(
                    "sync", lambda e, xb=xb, n=n, src=src: e.dma_start(
                        xb[:], src[n * 128:(n + 1) * 128, :]), inc=(sxl, 16))

            def router_ax(n):
                r.need("tensor", s["SV"], gmarks[n // 4])
                r.need("tensor", s["SV"], st9["r_read"])
                for dc in range(DC):
                    r.do("tensor", lambda e, dc=dc, n=n: e.matmul(
                        ps_r[:],
                        t["bigA"][:, dc * 2048 + n * 128:
                                  dc * 2048 + (n + 1) * 128],
                        r_sb[:, dc * 8:(dc + 1) * 8],
                        start=(dc == 0), stop=(dc == DC - 1)),
                        inc=(s["SP"], 1) if dc == DC - 1 else None)
                rmm = s["SP"].n
                r.need("vector", s["SP"], rmm)
                r.do("vector", lambda e: e.reduce_max(
                    top[:, 0:1], ps_r[:], axis=AX), inc=(s["SV"], 1))
                r.need("vector", s["SV"])
                r.do("vector", lambda e: e.tensor_scalar(
                    t1s[:], ps_r[:], top[:, 0:1], None, ALU.is_equal),
                    inc=(s["SV"], 1))
                r.need("vector", s["SV"])
                r.do("vector", lambda e: e.scalar_tensor_tensor(
                    mskd[:], t1s[:], -1e30, ps_r[:], ALU.mult, ALU.add),
                    inc=(s["SV"], 1))
                r.need("vector", s["SV"])
                r.do("vector", lambda e: e.reduce_max(
                    top[:, 1:2], mskd[:], axis=AX), inc=(s["SV"], 1))
                r.need("vector", s["SV"])
                r.do("vector", lambda e: e.tensor_scalar(
                    t2s[:], mskd[:], top[:, 1:2], None, ALU.is_equal),
                    inc=(s["SV"], 1))
                r.do("vector", lambda e: e.tensor_sub(
                    top[:, 2:3], top[:, 1:2], top[:, 0:1]), inc=(s["SV"], 1))
                r.need("scalar", s["SV"])
                r.do("scalar", lambda e: e.activation(
                    top[:, 3:4], top[:, 2:3], AF.Exp), inc=(s["SA"], 1))
                r.need("vector", s["SA"])
                r.do("vector", lambda e: e.tensor_scalar_add(
                    top[:, 4:5], top[:, 3:4], 1.0), inc=(s["SV"], 1))
                r.need("vector", s["SV"])
                r.do("vector", lambda e: e.reciprocal(
                    top[:, 5:6], top[:, 4:5]), inc=(s["SV"], 1))
                r.need("vector", s["SV"])
                r.do("vector", lambda e: e.tensor_mul(
                    top[:, 6:7], top[:, 3:4], top[:, 5:6]), inc=(s["SV"], 1))
                r.do("vector", lambda e: e.tensor_scalar(
                    td1[:], t1s[:], top[:, 5:6], None, ALU.mult),
                    inc=(s["SV"], 1))
                r.need("vector", s["SV"])
                r.need("vector", s["SP"], st9["wdense_free"])
                r.do("vector", lambda e: e.scalar_tensor_tensor(
                    wdense[:], t2s[:], top[:, 6:7], td1[:],
                    ALU.mult, ALU.add), inc=(s["SV"], 1))
                st9["r_read"] = s["SV"].n

            def gating_pe(n):
                r.need("tensor", s["SV"], st9["ax_free"])
                for dc in range(DC):
                    r.do("tensor", lambda e, dc=dc, n=n: e.matmul(
                        ps_ax[:],
                        a_sb[:, dc * 128:(dc + 1) * 128],
                        t["bigA"][:, dc * 2048 + n * 128:
                                  dc * 2048 + (n + 1) * 128],
                        start=(dc == 0), stop=(dc == DC - 1)),
                        inc=(s["SP"], 1) if dc == DC - 1 else None)
                wd = st9["r_read"]
                r.need("tensor", s["SV"], wd)
                r.need("tensor", s["SV"], st9["wdt_free"])
                r.do("tensor", lambda e: e.transpose(
                    ps_wdT[:], wdense[:], t["ident_bf"][:]), inc=(s["SP"], 1))
                st9["wdense_free"] = s["SP"].n
                r.need("vector", s["SP"], s["SP"].n)
                r.do("vector", lambda e: e.tensor_copy(wdT_sb[:], ps_wdT[:]),
                     inc=(s["SV"], 1))
                st9["wdt_free"] = s["SV"].n
                r.need("tensor", s["SV"], st9["wdt_free"])
                r.need("tensor", s["SV"], st9["wrep_free"])
                r.do("tensor", lambda e: e.matmul(
                    ps_wrep[:], exp8[:], wdT_sb[:],
                    start=True, stop=True), inc=(s["SP"], 1))
                wrepm = s["SP"].n
                r.need("vector", s["SP"], wrepm)
                r.do("vector", lambda e: e.tensor_copy(wrep_sb[:], ps_wrep[:]),
                     inc=(s["SV"], 1))
                st9["wrep_free"] = s["SV"].n
                gp = n % 2
                gsb = g_sb[gp]
                r.need("vector", s["SV"], s["SV"].n)
                r.need("vector", s["SP"], st9["g_free"][gp])
                r.do("vector", lambda e, gsb=gsb: e.tensor_tensor(
                    gsb[:], ps_ax[:], wrep_sb[:], ALU.mult), inc=(s["SV"], 1))
                st9["g_mark"][gp] = s["SV"].n
                st9["ax_free"] = s["SV"].n

            x_load(0)
            router_ax(0)
            gating_pe(0)
            for n in range(NST):
                par = n % 2
                xb = xtile[par]
                if n + 1 < NST:
                    x_load(n + 1)
                    router_ax(n + 1)
                gp = n % 2
                gsb = g_sb[gp]
                xn = xnew[0]
                sxn = s["SXN"][0]
                r.need("vector", sxn, xn_free[0])
                for ot in range(4):
                    ps = ps_f[ot]
                    r.need("tensor", s["SV"], ps_free[ot])
                    for dc in range(DC):
                        wap = wsel(dc, ot * 512, (ot + 1) * 512)
                        r.do("tensor", lambda e, ps=ps, dc=dc, n=n, wap=wap:
                             e.matmul(ps[:],
                                      t["bigA"][:, dc * 2048 + n * 128:
                                                dc * 2048 + (n + 1) * 128],
                                      wap,
                                      start=(dc == 0), stop=False))
                    r.need("tensor", s["SV"], st9["g_mark"][gp])
                    r.do("tensor", lambda e, ps=ps, gsb=gsb, ot=ot: e.matmul(
                        ps[:], gsb[:], b_sb[:, ot * 512:(ot + 1) * 512],
                        start=False, stop=True), inc=(s["SP"], 1))
                    mm = s["SP"].n
                    r.need("vector", s["SP"], mm)
                    r.need("vector", s["SXL"][par], st9["xl"][par])
                    ev = r.do("vector", lambda e, xn=xn, ps=ps, xb=xb, ot=ot:
                              e.tensor_tensor(xn[:, ot * 512:(ot + 1) * 512],
                                              ps[:],
                                              xb[:, ot * 512:(ot + 1) * 512],
                                              ALU.add), inc=(s["SV"], 1))
                    ps_free[ot] = ev
                st9["g_free"][gp] = s["SP"].n
                if n + 1 < NST:
                    gating_pe(n + 1)
                x_free[par] = s["SV"].n
                r.need("sync", s["SV"], s["SV"].n)
                stm = r.do("sync", lambda e, xn=xn, n=n, dst=dst: e.dma_start(
                    dst[n * 128:(n + 1) * 128, :], xn[:]), inc=(sxn, 16))
                xn_free[0] = stm
        return [(s["SXN"][i], s["SXN"][i].n) for i in range(2)]



def build_nc(n_sub=4, dbg=False):
    nc = bass.Bass("TRN2", target_bir_lowering=False)
    dram_in = {}

    def din(name, shape):
        dram_in[name] = nc.dram_tensor(name, list(shape), F32,
                                       kind="ExternalInput").ap()
        return dram_in[name]

    x_in = din("x_own", (NTOK, D))
    for nm_ in ("aw_q", "aw_k", "aw_v", "aw_o", "bw_q", "bw_k", "bw_v", "bw_o",
                "fw_t"):
        din(nm_, (128, DC * 2048))
    for nm_ in ("as_q", "as_k", "as_v", "as_o"):
        din(nm_, (DC, 4, 2048))
    din("ab_qk", (128, 32))
    din("ab_vrow", (1, 2048))
    din("ab_orow", (1, 2048))
    din("bb_qk", (128, 32))
    din("bb_vrow", (1, 2048))
    din("bb_orow", (1, 2048))
    din("fw_a", (128, 2048))
    din("fw_b", (128, 2048))
    din("fw_r", (128, 128))
    din("c_ident", (128, 128))
    din("c_msec0", (128, 128))
    din("c_msec1", (128, 128))
    din("c_exp8", (8, 128))

    y = nc.dram_tensor("y_own", [NTOK, D], F32, kind="ExternalOutput").ap()
    x_cur = nc.dram_tensor("x_cur", [NTOK, D], F32, kind="Internal").ap()
    q_dram = nc.dram_tensor("q_dram", [D, NTOK], BF16, kind="Internal").ap()
    kv_own = nc.dram_tensor("kv_own", [2 * NTOK, D], BF16, kind="Internal")
    kvf_K = nc.dram_tensor("kvf_K", [2 * NTOK, D], BF16, kind="Internal")
    kvf_V = nc.dram_tensor("kvf_V", [2 * NTOK, D], BF16, kind="Internal")

    rec = Rec()
    from contextlib import ExitStack
    with ExitStack() as stack:
        e = stack.enter_context
        sems = {}
        for nm_ in ("SP", "SV", "SA", "SCK", "SCV", "SWC", "SCN", "SCNW"):
            sems[nm_] = Sem(e(nc.semaphore(nm_)), nm_)
        sems["SV4"] = [Sem(e(nc.semaphore(f"SV4{i}")), f"SV4{i}")
                       for i in range(2)]
        for nm_, cnt in (("SXL", 2), ("SXN", 2), ("SSR", NSREP), ("SKQ", 2),
                         ("STG", 4), ("SWH", 3), ("SQG", 2)):
            sems[nm_] = [Sem(e(nc.semaphore(f"{nm_}{i}")), f"{nm_}{i}")
                         for i in range(cnt)]

        t = {}
        t["bigA"] = e(nc.sbuf_tensor("bigA", [128, DC * 2048], BF16))
        t["bigW"] = e(nc.sbuf_tensor("bigW", [128, DC * 2048], BF16))
        t["pbuf"] = e(nc.sbuf_tensor("pbuf", [128, 8 * 2048], BF16))
        t["srep"] = [e(nc.sbuf_tensor(f"srep{i}", [128, 2048], BF16))
                     for i in range(NSREP)]
        t["ident_bf"] = e(nc.sbuf_tensor("ident_bf", [128, 128], BF16))
        t["mask_sec"] = [e(nc.sbuf_tensor(f"msec{i}", [128, 128], BF16))
                         for i in range(2)]
        t["ones_col"] = e(nc.sbuf_tensor("ones_col", [128, 1], BF16))
        t["q_dram"] = q_dram
        t["exp8_dram"] = dram_in["c_exp8"]
        t["ident_dram"] = dram_in["c_ident"]
        t["kvf_K"] = kvf_K.ap()
        t["kvf_V"] = kvf_V.ap()

        bld = Build(nc, rec, sems, t)
        r, s = rec, sems

        r.do("gpsimd", lambda e: e.dma_start(t["ident_bf"][:],
                                             dram_in["c_ident"]),
             inc=(s["SCNW"], 16))
        r.do("gpsimd", lambda e: e.dma_start(t["mask_sec"][0][:],
                                             dram_in["c_msec0"]),
             inc=(s["SCNW"], 16))
        r.do("gpsimd", lambda e: e.dma_start(t["mask_sec"][1][:],
                                             dram_in["c_msec1"]),
             inc=(s["SCNW"], 16))
        r.do("vector", lambda e: e.memset(t["ones_col"][:], 1.0),
             inc=(s["SV"], 1))
        r.need("tensor", s["SCN"])
        r.need("tensor", s["SCNW"])
        r.need("tensor", s["SV"])
        r.need("scalar", s["SCN"])
        r.need("scalar", s["SCNW"])
        r.need("vector", s["SCN"])
        r.need("vector", s["SCNW"])

        def load_biases(pref, b_qk):
            r.need("sync", s["SP"])
            r.do("sync", lambda e, b_qk=b_qk: e.dma_start(
                b_qk[:], dram_in[pref + "b_qk"]), inc=(s["SCN"], 16))
            return s["SCN"].n

        def attn_core(pref, is_a, src):
            wp = "a" if is_a else "b"
            with ExitStack() as _as:
                _ae = _as.enter_context
                ps_p = [_ae(nc.psum_tensor(bld.nm("ps_p"), [128, 512], F32))
                        for _ in range(3)]
                b_qk = _ae(nc.sbuf_tensor(bld.nm("bqk"), [128, 32], F32))
                stage = [_ae(nc.sbuf_tensor(bld.nm("sgp"), [128, 512], BF16))
                         for _ in range(4)]
                t["b_qk"] = b_qk
                share = {"ps": [None, None, None], "stg": [0, 0, 0, 0]}
                wsel, ready, ab = bld.load_weight(
                    dram_in[wp + "w_k"], dram_in["as_k"] if is_a else None)
                bias_m = load_biases(pref, b_qk)
                gmarks = bld.ln_phase(src)
                ln_sa, ln_sv = s["SA"].n, s["SV"].n
                r.need("scalar", s["SCN"], bias_m)
                r.need("tensor", s["SCN"], bias_m)
                bld.proj_fm(ps_p, stage, gmarks, wsel, ready,
                            t["b_qk"][:, 16:32], kv_own.ap(), 0, share)
                bld.wring_release(ab)

                def k_coll():
                    for sg in s["STG"]:
                        r.need("gpsimd", sg, sg.n)
                    for ci in range(4):
                        r.do("gpsimd", lambda e, ci=ci: e.collective_compute(
                            "AllGather", ALU.bypass,
                            replica_groups=[[0, 1], [2, 3], [4, 5], [6, 7]],
                            ins=[kv_own.ap()[ci * 512:(ci + 1) * 512,
                                             :].opt()],
                            outs=[kvf_K.ap()[ci * 1024:(ci + 1) * 1024,
                                             :].opt()]),
                            inc=(s["SCK"], 1))
                wsel, ready, ab = bld.load_weight(
                    dram_in[wp + "w_v"], dram_in["as_v"] if is_a else None,
                    mid=k_coll)
                wsel_q, ready_q, ab_q, fin_q = bld.load_weight(
                    dram_in[wp + "w_q"], dram_in["as_q"] if is_a else None,
                    defer_b=True)
                bld.proj_tm(ps_p, stage, gmarks, wsel, ready,
                            dram_in[pref + "b_vrow"], kv_own.ap(), NTOK,
                            share, (ln_sa, ln_sv))
                bld.wring_release(ab)

                def v_coll():
                    for sg in s["STG"]:
                        r.need("gpsimd", sg, sg.n)
                    for ci in range(4):
                        r.do("gpsimd", lambda e, ci=ci: e.collective_compute(
                            "AllGather", ALU.bypass,
                            replica_groups=[[0, 1], [2, 3], [4, 5], [6, 7]],
                            ins=[kv_own.ap()[NTOK + ci * 512:
                                             NTOK + (ci + 1) * 512,
                                             :].opt()],
                            outs=[kvf_V.ap()[ci * 1024:(ci + 1) * 1024,
                                             :].opt()]),
                            inc=(s["SCV"], 1))
                ready_q = fin_q(v_coll)
                q_marks = bld.proj_fm(ps_p, stage, gmarks, wsel_q, ready_q,
                                      t["b_qk"][:, 0:16], q_dram, 0, share)
                bld.wring_release(ab_q)
            wo_sel, wo_ready, wo_ab = bld.load_weight(
                dram_in[wp + "w_o"], dram_in["as_o"] if is_a else None)
            halves = [(t["bigW"], 0), (t["bigW"], 8 * 2048), (t["pbuf"], 0)]
            fh = 3 - wo_ab[0] - wo_ab[1]
            ften, foff = halves[fh]
            v2b_ap = ften[:, foff:foff + 8192]
            bld.attention(not is_a, q_marks, v2b_ap)
            bld.wring_free[fh] = s["SP"].n
            return wo_sel, wo_ready, wo_ab

        def attn_sublayer(pref, is_a, src, dst):
            wo_sel, wo_ready, wo_ab = attn_core(pref, is_a, src)
            bld.barrier()
            out_marks = bld.oproj_phase(wo_sel, wo_ready,
                                        dram_in[pref + "b_orow"], src, dst)
            bld.wring_release(wo_ab)
            bld.barrier()
            return out_marks

        def ffn_sublayer(src, dst):
            with ExitStack() as _fs:
                a_sb = _fs.enter_context(
                    nc.sbuf_tensor(bld.nm("a_sb"), [128, 2048], BF16))
                b_sb = _fs.enter_context(
                    nc.sbuf_tensor(bld.nm("b_sb"), [128, 2048], BF16))
                r_sb = _fs.enter_context(
                    nc.sbuf_tensor(bld.nm("r_sb"), [128, 128], BF16))
                wsel, ready, ab = bld.load_weight(dram_in["fw_t"], None)
                r.need("gpsimd", s["SP"])
                r.do("gpsimd", lambda e, a_sb=a_sb: e.dma_start(
                    a_sb[:], dram_in["fw_a"]), inc=(s["SWC"], 16))
                r.do("gpsimd", lambda e, b_sb=b_sb: e.dma_start(
                    b_sb[:], dram_in["fw_b"]), inc=(s["SWC"], 16))
                m3 = r.do("gpsimd", lambda e, r_sb=r_sb: e.dma_start(
                    r_sb[:], dram_in["fw_r"]), inc=(s["SWC"], 16))
                gmarks = bld.ln_phase(src)
                ready2 = list(ready) + [(s["SWC"], m3)]
                out_marks = bld.ffn_phase(wsel, ready2, src, dst,
                                          (a_sb, b_sb, r_sb), gmarks)
                bld.wring_release(ab)
                bld.barrier()
            return out_marks

        final_marks = []
        if n_sub < 0:
            bld.ln_phase(x_in)
            _dbstk = ExitStack()
            if n_sub <= -2:
                with nc.psum_tensor(bld.nm("ps_p0"), [128, 512], F32) as pp0, \
                        nc.psum_tensor(bld.nm("ps_p1"), [128, 512], F32) as pp1, \
                        nc.psum_tensor(bld.nm("ps_p2"), [128, 512], F32) as pp2:
                    ps_p = [pp0, pp1, pp2]
                    b_qk = _dbstk.enter_context(
                        nc.sbuf_tensor(bld.nm("bqk"), [128, 32], F32))
                    stage = [_dbstk.enter_context(nc.sbuf_tensor(
                        bld.nm("sgp"), [128, 512], BF16)) for _ in range(4)]
                    t["b_qk"] = b_qk
                    bias_m = load_biases("a", b_qk)
                    wsel, ready, ab = bld.load_weight(dram_in["aw_k"],
                                                      dram_in["as_k"])
                    gmarks = [s["SV"].n] * 4
                    r.need("scalar", s["SCN"], bias_m)
                    r.need("tensor", s["SCN"], bias_m)
                    share = {"ps": [None, None, None], "stg": [0, 0, 0, 0]}
                    bld.proj_fm(ps_p, stage, gmarks, wsel, ready,
                                t["b_qk"][:, 16:32], kv_own.ap(), 0, share)
                    bld.wring_release(ab)
                    if n_sub <= -3:
                        wsel, ready, ab = bld.load_weight(dram_in["aw_v"],
                                                          dram_in["as_v"])
                        bld.proj_tm(ps_p, stage, gmarks, wsel, ready,
                                    dram_in["ab_vrow"], kv_own.ap(), NTOK,
                                    share)
                        bld.wring_release(ab)
            bld.barrier()

        seq = []
        if n_sub >= 1:
            seq.append(("attn", "a", x_in, x_cur if n_sub > 1 else y))
        if n_sub >= 2:
            seq.append(("ffn", None, x_cur, x_cur if n_sub > 2 else y))
        if n_sub >= 3:
            seq.append(("attn", "b", x_cur, x_cur if n_sub > 3 else y))
        if n_sub >= 4:
            seq.append(("ffn", None, x_cur, y))

        for kind, which, src, dst in seq:
            if kind == "attn":
                final_marks = attn_sublayer(which, which == "a", src, dst)
            else:
                final_marks = ffn_sublayer(src, dst)

        for sem, val in final_marks:
            r.need("sync", sem, val)

        if dbg:
            dq = nc.dram_tensor("dbg_q", [D, NTOK], BF16,
                                kind="ExternalOutput").ap()
            dkv = nc.dram_tensor("dbg_kv", [2 * NTOK, D], BF16,
                                 kind="ExternalOutput").ap()
            dkvf = nc.dram_tensor("dbg_kvf", [4 * NTOK, D], BF16,
                                  kind="ExternalOutput").ap()
            dba = nc.dram_tensor("dbg_bigA", [128, DC * 2048], BF16,
                                 kind="ExternalOutput").ap()
            dbw = nc.dram_tensor("dbg_bigW", [128, DC * 2048], BF16,
                                 kind="ExternalOutput").ap()
            bld.barrier()
            r.do("sync", lambda e: e.dma_start(dq, q_dram), inc=(s["SCN"], 16))
            r.do("sync", lambda e: e.dma_start(dkv, kv_own.ap()),
                 inc=(s["SCN"], 16))
            r.do("sync", lambda e: e.dma_start(dkvf, kv_full.ap()),
                 inc=(s["SCN"], 16))
            r.do("sync", lambda e: e.dma_start(dba, t["bigA"][:]),
                 inc=(s["SCN"], 16))
            r.do("sync", lambda e: e.dma_start(dbw, t["bigW"][:]),
                 inc=(s["SCN"], 16))
            r.need("sync", s["SCN"])

        with nc.Block() as block:
            @block.sync
            def _(eng):
                for fn in rec.q["sync"]:
                    fn(eng)

            @block.tensor
            def _(eng):
                for fn in rec.q["tensor"]:
                    fn(eng)

            @block.vector
            def _(eng):
                for fn in rec.q["vector"]:
                    fn(eng)

            @block.scalar
            def _(eng):
                for fn in rec.q["scalar"]:
                    fn(eng)

            @block.gpsimd
            def _(eng):
                for fn in rec.q["gpsimd"]:
                    fn(eng)

    return nc



def _chunk_interleave(w):
    return np.ascontiguousarray(
        w.reshape(DC, 128, 2048).transpose(1, 0, 2).reshape(128, DC * 2048))


def _bias_pack(b):
    return np.ascontiguousarray(np.asarray(b, np.float32).reshape(DC, 128).T)


def make_core_inputs(core, I):
    b, p = core // 2, core % 2
    rows = np.arange(NTOK)
    gidx = (2 * (rows // 128) + p) * 128 + rows % 128
    x = np.ascontiguousarray(np.asarray(I["L0_attn_a_x"], np.float32)[b][gidx])

    d = {"x_own": x}
    for nm_, key in (("q", "wq"), ("k", "wk"), ("v", "wv"), ("o", "wo")):
        d[f"aw_{nm_}"] = _chunk_interleave(
            np.asarray(I[f"L0_attn_a_{key}_codes"], np.float32))
        d[f"as_{nm_}"] = np.ascontiguousarray(
            np.asarray(I[f"L0_attn_a_{key}_scales"], np.float32)
            .reshape(DC, 4, 2048))
    d["ab_qk"] = np.concatenate(
        [_bias_pack(I["L0_attn_a_bq"]), _bias_pack(I["L0_attn_a_bk"])], axis=1)
    d["ab_vrow"] = np.asarray(I["L0_attn_a_bv"], np.float32).reshape(1, 2048)
    d["ab_orow"] = np.asarray(I["L0_attn_a_bo"], np.float32).reshape(1, 2048)
    for nm_, key in (("q", "Wq"), ("k", "Wk"), ("v", "Wv")):
        w = np.asarray(I[f"L0_attn_b_{key}"], np.float32).transpose(1, 0, 2)
        d[f"bw_{nm_}"] = _chunk_interleave(
            np.ascontiguousarray(w.reshape(D, 2048)))
    d["bw_o"] = _chunk_interleave(np.asarray(I["L0_attn_b_Wo"], np.float32))
    d["bb_qk"] = np.concatenate(
        [_bias_pack(np.asarray(I["L0_attn_b_bq"], np.float32).reshape(-1)),
         _bias_pack(np.asarray(I["L0_attn_b_bk"], np.float32).reshape(-1))],
        axis=1)
    d["bb_vrow"] = np.asarray(I["L0_attn_b_bv"], np.float32).reshape(1, 2048)
    d["bb_orow"] = np.asarray(I["L0_attn_b_bo"], np.float32).reshape(1, 2048)
    d["fw_t"] = _chunk_interleave(
        np.ascontiguousarray(np.asarray(I["L0_ffn_weight"], np.float32).T))
    d["fw_a"] = np.ascontiguousarray(
        np.asarray(I["L0_ffn_lora_A"], np.float32).transpose(2, 0, 1)
        .reshape(D, 128).reshape(DC, 128, 128).transpose(1, 0, 2)
        .reshape(128, 2048))
    d["fw_b"] = np.ascontiguousarray(
        np.asarray(I["L0_ffn_lora_B"], np.float32).transpose(0, 2, 1)
        .reshape(128, D))
    d["fw_r"] = np.ascontiguousarray(
        np.asarray(I["L0_ffn_router_w"], np.float32).T.reshape(DC, 128, 8)
        .transpose(1, 0, 2).reshape(128, 128))
    d["c_ident"] = np.eye(128, dtype=np.float32)
    tri = np.triu(np.ones((128, 128), np.float32))
    bound = (np.ones if p == 1 else np.zeros)((128, 128), np.float32)
    d["c_msec0"] = tri if p == 0 else bound
    d["c_msec1"] = bound if p == 0 else tri
    ex = np.zeros((8, 128), np.float32)
    for e_ in range(8):
        ex[e_, e_ * 16:(e_ + 1) * 16] = 2.0
    d["c_exp8"] = ex
    return d


_NC_CACHE = {}


def kernel(**inputs):
    n_sub = int(inputs.pop("_n_sub", 4))
    dbg = bool(inputs.pop("_dbg", False))
    key = (n_sub, dbg)
    if key not in _NC_CACHE:
        _NC_CACHE[key] = build_nc(n_sub, dbg)
    nc = _NC_CACHE[key]
    in_maps = [make_core_inputs(c, inputs) for c in range(NCORES)]
    res = run_bass_kernel_spmd(nc, in_maps, core_ids=list(range(NCORES)))
    if dbg:
        kernel._dbg_res = res
    out = np.zeros((B, S, D), np.float32)
    for c in range(NCORES):
        b, p = c // 2, c % 2
        rows = np.arange(NTOK)
        gidx = (2 * (rows // 128) + p) * 128 + rows % 128
        out[b][gidx] = np.asarray(res.results[c]["y_own"], np.float32)
    return out


if __name__ == "__main__":
    nc = build_nc(int(sys.argv[1]) if len(sys.argv) > 1 else 4)
    print("built ok")


# version 20
# speedup vs baseline: 1.0482x; 1.0106x over previous
import sys
import numpy as np

sys.path.insert(0, "/opt/trn_rl_repo")

import concourse.bass as bass
import concourse.mybir as mybir
from concourse.bass_utils import run_bass_kernel_spmd

F32 = mybir.dt.float32
BF16 = mybir.dt.bfloat16
AF = mybir.ActivationFunctionType
ALU = mybir.AluOpType
AX = mybir.AxisListType.X

NCORES = 8
B, S, D = 4, 4096, 2048
H, DH = 16, 128
NTOK = 2048
NST = 16
DC = 16
EPS = 1e-5
ISQDH = float(1.0 / np.sqrt(DH))

ENGINES = ("sync", "tensor", "vector", "scalar", "gpsimd")


class Sem:
    __slots__ = ("h", "n", "name")

    def __init__(self, h, name):
        self.h = h
        self.n = 0
        self.name = name


class Rec:
    def __init__(self):
        self.q = {e: [] for e in ENGINES}
        self.waited = {e: {} for e in ENGINES}
        self.ninstr = 0

    def do(self, eng, make, inc=None):
        if inc is not None:
            sem, amt = inc
            sem.n += amt
            h, a = sem.h, amt

            def fn(e, make=make, h=h, a=a):
                make(e).then_inc(h, a)
        else:
            def fn(e, make=make):
                make(e)
        self.q[eng].append(fn)
        self.ninstr += 1
        return inc[0].n if inc is not None else None

    def need(self, eng, sem, val=None):
        if val is None:
            val = sem.n
        if val <= 0:
            return
        w = self.waited[eng]
        if w.get(sem.name, -1) >= val:
            return
        w[sem.name] = val
        self.q[eng].append(lambda e, h=sem.h, v=val: e.wait_ge(h, v))
        self.ninstr += 1


class Build:
    def __init__(self, nc, rec, sems, t):
        self.nc = nc
        self.r = rec
        self.s = sems
        self.t = t
        self.uid = 0
        self.all_sems = [v for v in sems.values() if isinstance(v, Sem)]
        for v in sems.values():
            if isinstance(v, list):
                self.all_sems.extend(v)

    def nm(self, base):
        self.uid += 1
        return f"{base}_{self.uid}"

    def barrier(self):
        for e in ENGINES:
            if e == "gpsimd":
                continue
            for s_ in self.all_sems:
                self.r.need(e, s_, s_.n)

    def entry_wait(self, eng="tensor"):
        for e in (eng, "sync"):
            self.r.need(e, self.s["SV"])
            self.r.need(e, self.s["SA"])
            self.r.need(e, self.s["SP"])

    def ln_phase(self, src):
        r, s, t = self.r, self.s, self.t
        nc = self.nc
        from contextlib import ExitStack
        with ExitStack() as _stk:
            _e = _stk.enter_context
            ps_t = [_e(nc.psum_tensor(self.nm("ps_t"), [128, 512], BF16))
                    for _ in range(2)]
            xtile = [_e(nc.sbuf_tensor(self.nm("xt"), [128, 2048], F32))
                     for _ in range(2)]
            xln_tm = [_e(nc.sbuf_tensor(self.nm("xm"), [128, 2048], BF16))
                      for _ in range(2)]
            stats = [_e(nc.sbuf_tensor(self.nm("sst"), [128, 24], F32))
                     for _ in range(2)]
            mvs = [_e(nc.sbuf_tensor(self.nm("mvv"), [128, 2], F32))
                   for _ in range(2)]
            lnts = [_e(nc.sbuf_tensor(self.nm("lt"), [128, 3], F32))
                    for _ in range(2)]
            self.entry_wait("tensor")
            gmarks = []
            apply_done = [0, 0]
            tp_done = [0, 0]
            evac_done = [0, 0]
            for st in range(NST):
                par = st % 2
                xb = xtile[par]
                sxl = s["SXL"][par]
                r.need("sync", s["SA"], apply_done[par])
                ld = r.do("sync", lambda e, xb=xb, st=st, src=src: e.dma_start(
                    xb[:], src[st * 128:(st + 1) * 128, :]), inc=(sxl, 16))
                r.need("vector", sxl, ld)
                r.need("vector", s["SA"], apply_done[par])
                stt = stats[par]
                for i in range(4):
                    inc = (s["SV"], 1) if i == 3 else None
                    r.do("vector", lambda e, i=i, xb=xb, stt=stt: e.bn_stats(
                        stt[:, 6 * i:6 * i + 6], xb[:, 512 * i:512 * (i + 1)]),
                        inc=inc)
                r.need("vector", s["SV"])
                mv = mvs[par]
                r.do("vector", lambda e, mv=mv, stt=stt: e.bn_aggr(mv[:], stt[:]),
                     inc=(s["SV"], 1))
                r.need("vector", s["SV"])
                lnt = lnts[par]
                r.do("vector", lambda e, lnt=lnt, mv=mv: e.tensor_scalar_add(
                    lnt[:, 0:1], mv[:, 1:2], EPS), inc=(s["SV"], 1))
                r.need("scalar", s["SV"])
                r.do("scalar", lambda e, lnt=lnt: e.activation(
                    lnt[:, 0:1], lnt[:, 0:1], AF.Ln), inc=(s["SA"], 1))
                r.need("scalar", s["SA"])
                r.do("scalar", lambda e, lnt=lnt: e.activation(
                    lnt[:, 1:2], lnt[:, 0:1], AF.Exp, scale=-0.5),
                    inc=(s["SA"], 1))
                r.need("vector", s["SA"])
                r.do("vector", lambda e, lnt=lnt, mv=mv: e.scalar_tensor_tensor(
                    lnt[:, 2:3], mv[:, 0:1], -1.0, lnt[:, 1:2],
                    ALU.mult, ALU.mult), inc=(s["SV"], 1))
                xm = xln_tm[par]
                r.need("scalar", s["SV"])
                r.need("scalar", s["SP"], tp_done[par])
                av = r.do("scalar", lambda e, xm=xm, xb=xb, lnt=lnt: e.activation(
                    xm[:], xb[:], AF.Identity, bias=lnt[:, 2:3],
                    scale=lnt[:, 1:2]), inc=(s["SA"], 1))
                apply_done[par] = av
                r.need("tensor", s["SA"], av)
                for g4 in range(4):
                    p2 = g4 % 2
                    pst = ps_t[p2]
                    r.need("tensor", s["SV"], evac_done[p2])
                    for k in range(4):
                        dc = g4 * 4 + k
                        inc = (s["SP"], 1) if k == 3 else None
                        r.do("tensor", lambda e, pst=pst, k=k, xm=xm, dc=dc:
                             e.transpose(pst[:, k * 128:(k + 1) * 128],
                                         xm[:, dc * 128:(dc + 1) * 128],
                                         t["ident_bf"][:]), inc=inc)
                    tpv = s["SP"].n
                    if g4 == 3:
                        tp_done[par] = tpv
                    r.need("vector", s["SP"], tpv)
                    dst = t["bigA"][:].rearrange("p (dc n) -> p dc n", n=NTOK)[
                        :, g4 * 4:(g4 + 1) * 4, st * 128:(st + 1) * 128]
                    srcap = pst[:].rearrange("p (k n) -> p k n", n=128)
                    ev = r.do("vector", lambda e, dst=dst, srcap=srcap:
                              e.tensor_copy(dst, srcap), inc=(s["SV"], 1))
                    evac_done[p2] = ev
                if st % 4 == 3:
                    gmarks.append(s["SV"].n)
        return gmarks

    def load_weight(self, codes, scales):
        r, s, t = self.r, self.s, self.t
        if not hasattr(self, "wring_free"):
            self.wring_free = [0, 0, 0]
            self.wring_a = 0
            self.srep_free = [0, 0]
        a = self.wring_a
        b = (a + 1) % 3
        self.wring_a = (a + 2) % 3
        halves = [(t["bigW"], 0), (t["bigW"], 8 * 2048), (t["pbuf"], 0)]

        def wsel(dc, c0, c1, a=a, b=b, halves=halves):
            h = a if dc < 8 else b
            ten, off = halves[h]
            base = off + (dc % 8) * 2048
            return ten[:, base + c0: base + c1]

        ready = []
        for h, src_col in ((a, 0), (b, 8 * 2048)):
            r.need("gpsimd", s["SP"], self.wring_free[h])
            ten, off = halves[h]
            m = 0
            for part in range(4):
                m = r.do("gpsimd", lambda e, ten=ten, off=off, part=part,
                         src_col=src_col, codes=codes: e.dma_start(
                             ten[part * 32:(part + 1) * 32,
                                 off: off + 8 * 2048],
                             codes[part * 32:(part + 1) * 32,
                                   src_col: src_col + 8 * 2048]),
                         inc=(s["SWH"][h], 16))
            ready.append((s["SWH"][h], m))
        if scales is not None:
            for dc in range(DC):
                par = dc % 2
                sr = t["srep"][par]
                ssr = s["SSR"][par]
                h = a if dc < 8 else b
                r.need("gpsimd", s["SV"], self.srep_free[par])
                srcap = scales[dc:dc + 1, :, :].rearrange("dc g f -> g dc f")
                srcap = srcap.unsqueeze(1).broadcast_to((4, 32, 1, 2048))
                m = 0
                for ai in range(4):
                    m = r.do("gpsimd", lambda e, sr=sr, srcap=srcap, ai=ai:
                             e.dma_start(sr[ai * 32:(ai + 1) * 32, :],
                                         srcap[ai:ai + 1]), inc=(ssr, 16))
                r.need("vector", ssr, m)
                r.need("vector", s["SWH"][h], dict(ready)[s["SWH"][h]])
                dst = wsel(dc, 0, 2048)
                ev = r.do("vector", lambda e, dst=dst, sr=sr: e.tensor_tensor(
                    dst, dst, sr[:], ALU.mult), inc=(s["SV"], 1))
                self.srep_free[par] = ev
            ready.append((s["SV"], s["SV"].n))
        return wsel, ready, (a, b)

    def wring_release(self, ab):
        a, b = ab
        self.wring_free[a] = self.s["SP"].n
        self.wring_free[b] = self.s["SP"].n

    def proj_fm(self, ps_p, stage, gmarks, wsel, ready, bcol_ap, dst_dram,
                dst_row0, share):
        r, s, t = self.r, self.s, self.t
        psfree = share["ps"]
        stfree = share["stg"]
        k = 0
        for sem, val in ready:
            r.need("tensor", sem, val)
        for g in range(4):
            r.need("tensor", s["SV"], gmarks[g])
            for f in range(DC):
                ps = ps_p[k % 3]
                if psfree[k % 3] is not None:
                    r.need("tensor", psfree[k % 3][0], psfree[k % 3][1])
                for dc in range(DC):
                    inc = (s["SP"], 1) if dc == DC - 1 else None
                    wap = wsel(dc, f * 128, (f + 1) * 128)
                    r.do("tensor", lambda e, ps=ps, dc=dc, g=g, wap=wap:
                         e.matmul(
                             ps[:], wap,
                             t["bigA"][:, dc * 2048 + g * 512:
                                       dc * 2048 + (g + 1) * 512],
                             start=(dc == 0), stop=(dc == DC - 1)),
                         inc=inc)
                mm = s["SP"].n
                slot = k % 4
                stg = stage[slot]
                sst = s["STG"][slot]
                r.need("scalar", s["SP"], mm)
                r.need("scalar", sst, stfree[slot])
                ev = r.do("scalar", lambda e, stg=stg, ps=ps, f=f:
                          e.activation(stg[:], ps[:], AF.Identity,
                                       bias=bcol_ap[:, f:f + 1]),
                          inc=(s["SA"], 1))
                psfree[k % 3] = (s["SA"], ev)
                r.need("sync", s["SA"], ev)
                st_ = r.do("sync", lambda e, stg=stg, dst_dram=dst_dram,
                           f=f, g=g: e.dma_start(dst_dram[
                               dst_row0 + f * 128: dst_row0 + (f + 1) * 128,
                               g * 512:(g + 1) * 512], stg[:]),
                           inc=(sst, 16))
                stfree[slot] = st_
                k += 1
        return [(s["STG"][i], s["STG"][i].n) for i in range(4)]

    def proj_tm(self, ps_p, stage, gmarks, wsel, ready, brow_dram, dst_dram,
                dst_row0, share, ln_marks=None):
        r, s, t = self.r, self.s, self.t
        psfree = share["ps"]
        stfree = share["stg"]
        k = 0
        for sem, val in ready:
            r.need("tensor", sem, val)
        from contextlib import ExitStack
        with ExitStack() as _stk:
            brow = _stk.enter_context(self.nc.sbuf_tensor(
                self.nm("brow"), [1, 2048], BF16))
            ones_row = _stk.enter_context(self.nc.sbuf_tensor(
                self.nm("onr"), [1, 128], BF16))
            brow_ap = brow[:]
            if ln_marks is not None:
                r.need("gpsimd", s["SA"], ln_marks[0])
                r.need("gpsimd", s["SV"], ln_marks[1])
            else:
                r.need("gpsimd", s["SP"])
            bm = r.do("gpsimd", lambda e, brow=brow, brow_dram=brow_dram:
                      e.dma_start(brow[:], brow_dram), inc=(s["SCNW"], 16))
            r.need("tensor", s["SCNW"], bm)
            r.need("vector", s["SA"])
            r.do("vector", lambda e, ones_row=ones_row: e.memset(
                ones_row[:], 1.0), inc=(s["SV"], 1))
            r.need("tensor", s["SV"])
            for n in range(DC):
                r.need("tensor", s["SV"], gmarks[n // 4])
                for ft in range(4):
                    ps = ps_p[k % 3]
                    if psfree[k % 3] is not None:
                        r.need("tensor", psfree[k % 3][0], psfree[k % 3][1])
                    for dc in range(DC):
                        wap = wsel(dc, ft * 512, (ft + 1) * 512)
                        r.do("tensor", lambda e, ps=ps, dc=dc, n=n, wap=wap:
                             e.matmul(ps[:],
                                      t["bigA"][:, dc * 2048 + n * 128:
                                                dc * 2048 + (n + 1) * 128],
                                      wap,
                                      start=(dc == 0), stop=False))
                    r.do("tensor", lambda e, ps=ps, ft=ft, brow_ap=brow_ap,
                         ones_row=ones_row:
                         e.matmul(ps[:], ones_row[:],
                                  brow_ap[:, ft * 512:(ft + 1) * 512],
                                  start=False, stop=True), inc=(s["SP"], 1))
                    mm = s["SP"].n
                    slot = k % 4
                    stg = stage[slot]
                    sst = s["STG"][slot]
                    r.need("vector", s["SP"], mm)
                    r.need("vector", sst, stfree[slot])
                    ev = r.do("vector", lambda e, stg=stg, ps=ps: e.tensor_copy(
                        stg[:], ps[:]), inc=(s["SV"], 1))
                    psfree[k % 3] = (s["SV"], ev)
                    r.need("sync", s["SV"], ev)
                    st_ = r.do("sync", lambda e, stg=stg, dst_dram=dst_dram,
                               n=n, ft=ft: e.dma_start(dst_dram[
                                   dst_row0 + n * 128: dst_row0 + (n + 1) * 128,
                                   ft * 512:(ft + 1) * 512], stg[:]),
                               inc=(sst, 16))
                    stfree[slot] = st_
                    k += 1
        return [(s["STG"][i], s["STG"][i].n) for i in range(4)]

    def attention(self, causal, q_marks, v2b_ap):
        r, s, t = self.r, self.s, self.t
        self.entry_wait("tensor")
        for sem, val in q_marks:
            r.need("sync", sem, val)
            r.need("scalar", sem, val)
        nc = self.nc
        from contextlib import ExitStack
        with ExitStack() as _stk:
            _e = _stk.enter_context
            ps_S = [_e(nc.psum_tensor(self.nm("ps_S"), [128, 1024], F32))
                    for _ in range(2)]
            ps_O = _e(nc.psum_tensor(self.nm("ps_O"), [128, 512], F32))
            ps_sums = _e(nc.psum_tensor(self.nm("ps_su"), [128, 4], F32))
            ps_T = _e(nc.psum_tensor(self.nm("ps_T"), [128, 512], BF16))
            V4sb = _e(nc.sbuf_tensor(self.nm("V2"), [128, 32 * 256], BF16))
            K_h = [_e(nc.sbuf_tensor(self.nm("kh"), [128, 4096], BF16))
                   for _ in range(2)]
            Qg = [_e(nc.sbuf_tensor(self.nm("qg"), [128, 512], BF16))
                  for _ in range(2)]
            ptil = [_e(nc.sbuf_tensor(self.nm("pt"), [128, 1024], BF16))
                    for _ in range(2)]
            Otm = _e(nc.sbuf_tensor(self.nm("Otm"), [128, 512], BF16))
            mtile = [Otm]
            recip = _e(nc.sbuf_tensor(self.nm("rcp"), [128, 4], F32))

            v2bufs = [V4sb[:], v2b_ap]
            sp0 = s["SP"].n
            st8 = {"kq_free": [0, 0], "v4_free": [sp0, sp0], "pend": None,
                   "pv_mark": [0, 0], "swar": [0, 0], "evT": 0,
                   "mt_free": [0, 0], "mt_par": 0, "v4m": [0, 0],
                   "v2cur": None, "qg_free": [0, 0]}

            def pv_block(jlist, reg, qg, hh, expm, epoch_first, final=False):
                vb, vpar_ = st8["v2cur"]
                r.need("tensor", s["SV4"][vpar_], st8["v4m"][vpar_])
                pairs = []
                for k, j in enumerate(jlist):
                    ls = j % 16
                    for qs in range(4):
                        if causal and ls > 4 * qg + qs:
                            continue
                        pairs.append((k, j, qs))
                for pi, (k, j, qs) in enumerate(pairs):
                    ls, sec = j % 16, j // 16
                    is_last = final and (pi == len(pairs) - 1)
                    if True:
                        stq = 4 * qg + qs
                        masked = causal and (ls == stq)
                        lhs_t = None
                        if masked:
                            mp = 0
                            mt = mtile[mp]
                            r.need("vector", s["SA"], expm)
                            r.need("vector", s["SP"], st8["mt_free"][mp])
                            r.do("vector", lambda e, mt=mt, reg=reg, k=k, qs=qs,
                                 sec=sec: e.tensor_tensor(
                                     mt[:, 0:128],
                                     ptil[reg][:, k * 512 + qs * 128:
                                               k * 512 + (qs + 1) * 128],
                                     t["mask_sec"][sec][:], ALU.mult),
                                 inc=(s["SV"], 1))
                            r.need("tensor", s["SV"], s["SV"].n)
                            lhs_t = (mt, mp)
                        r.need("tensor", s["SA"], expm)
                        stO = epoch_first[0]
                        epoch_first[0] = False

                        def lhs(e, lhs_t=lhs_t, reg=reg, k=k, qs=qs):
                            if lhs_t is not None:
                                return lhs_t[0][:, 0:128]
                            return ptil[reg][:, k * 512 + qs * 128:
                                             k * 512 + (qs + 1) * 128]

                        r.do("tensor", lambda e, lhs=lhs, j=j, qs=qs, hh=hh,
                             vb=vb, stO=stO, is_last=is_last: e.matmul(
                                 ps_O[:, qs * 128:(qs + 1) * 128], lhs(e),
                                 vb[:, j * 256 + hh * 128:
                                    j * 256 + (hh + 1) * 128],
                                 start=stO, stop=is_last))
                        stSu = epoch_first[1]
                        epoch_first[1] = False
                        mm = r.do("tensor", lambda e, lhs=lhs, qs=qs, stSu=stSu,
                                  is_last=is_last:
                                  e.matmul(ps_sums[:, qs:qs + 1], lhs(e),
                                           t["ones_col"][:], start=stSu,
                                           stop=is_last), inc=(s["SP"], 1))
                        if lhs_t is not None:
                            st8["mt_free"][lhs_t[1]] = mm

            for u in range(8):
                vpar = u % 2
                vbuf = v2bufs[vpar]
                st8["v2cur"] = (vbuf, vpar)
                r.need("scalar", s["SP"], st8["v4_free"][vpar])
                if u == 0:
                    r.need("scalar", s["SCV"])
                    r.need("sync", s["SCK"])
                for sec in range(2):
                    for ci in range(4):
                        row0 = ci * 1024 + sec * 512
                        vsrc = t["kvf_V"][row0:row0 + 512,
                                          u * 256:(u + 1) * 256]
                        vsrc = vsrc.rearrange("(ls p) f -> p ls f", p=128)
                        c0 = (sec * 16 + ci * 4) * 256
                        vdst = vbuf[:, c0:c0 + 4 * 256]
                        vdst = vdst.rearrange("p (ls f) -> p ls f", f=256)
                        st8["v4m"][vpar] = r.do(
                            "scalar", lambda e, vdst=vdst, vsrc=vsrc:
                            e.dma_start(vdst, vsrc), inc=(s["SV4"][vpar], 16))
                for hh in range(2):
                    h = 2 * u + hh
                    par = hh % 2
                    kh = K_h[par]
                    skq = s["SKQ"][par]
                    r.need("sync", s["SP"], st8["kq_free"][par])
                    qm_ = 0
                    ci_, hp_ = h // 4, h % 4
                    for sec in range(2):
                        row0 = ci_ * 1024 + sec * 512 + hp_ * 128
                        qm_ = r.do("sync", lambda e, kh=kh, sec=sec, row0=row0:
                                   e.dma_start(
                                       kh[:, sec * 2048:(sec + 1) * 2048],
                                       t["kvf_K"][row0:row0 + 128, :]),
                                   inc=(skq, 16))
                    r.need("tensor", skq, qm_)
                    for qg in range(4):
                        qsl = qg % 2
                        r.need("sync", s["SP"], st8["qg_free"][qsl])
                        qgm = r.do("sync", lambda e, qsl=qsl, h=h, qg=qg:
                                   e.dma_start(
                                       Qg[qsl][:],
                                       t["q_dram"][h * 128:(h + 1) * 128,
                                                   qg * 512:(qg + 1) * 512]),
                                   inc=(s["SQG"][qsl], 16))
                        r.need("tensor", s["SQG"][qsl], qgm)
                        epoch_first = [True, True]
                        r.need("tensor", s["SV"], s["SV"].n)
                        if causal:
                            included = [j for j in range(32)
                                        if (j % 16) <= 4 * qg + 3]
                        else:
                            included = list(range(32))
                        regions = [included[i:i + 2]
                                   for i in range(0, len(included), 2)]
                        pend_pv = None
                        for ri, jgrp in enumerate(regions):
                            reg = ri % 2
                            r.need("tensor", s["SA"], st8["swar"][reg])
                            psS = ps_S[reg]
                            qga = Qg[qsl]
                            for slot, j in enumerate(jgrp):
                                r.do("tensor", lambda e, psS=psS, slot=slot,
                                     kh=kh, qga=qga, j=j: e.matmul(
                                         psS[:, slot * 512:(slot + 1) * 512],
                                         kh[:, j * 128:(j + 1) * 128],
                                         qga[:],
                                         start=True, stop=True),
                                     inc=(s["SP"], 1))
                            nsl = len(jgrp)
                            r.need("scalar", s["SP"], s["SP"].n)
                            r.need("scalar", s["SP"], st8["pv_mark"][reg])
                            pt = ptil[reg]
                            em = r.do("scalar", lambda e, psS=psS, pt=pt,
                                      nsl=nsl: e.activation(
                                          pt[:, 0:nsl * 512],
                                          psS[:, 0:nsl * 512],
                                          AF.Exp, scale=ISQDH),
                                      inc=(s["SA"], 1))
                            st8["swar"][reg] = em
                            if st8["pend"] is not None:
                                st8["pend"]()
                                st8["pend"] = None
                            if pend_pv is not None:
                                pv_block(*pend_pv)
                                st8["pv_mark"][pend_pv[1]] = s["SP"].n
                            pend_pv = (jgrp, reg, qg, hh, em, epoch_first)
                        if pend_pv is not None:
                            pv_block(*pend_pv, final=True)
                            st8["pv_mark"][pend_pv[1]] = s["SP"].n
                            pend_pv = None
                        st8["qg_free"][qsl] = s["SP"].n
                        r.need("vector", s["SP"], s["SP"].n)
                        r.do("vector", lambda e: e.reciprocal(
                            recip[:], ps_sums[:, 0:4]), inc=(s["SV"], 1))
                        r.need("vector", s["SV"])
                        for qs in range(4):
                            inc = (s["SV"], 1) if qs == 3 else None
                            r.do("vector", lambda e, qs=qs: e.tensor_scalar(
                                Otm[:, qs * 128:(qs + 1) * 128],
                                ps_O[:, qs * 128:(qs + 1) * 128],
                                recip[:, qs:qs + 1], None, ALU.mult), inc=inc)
                        nd = s["SV"].n

                        def flushT(h=h, qg=qg, nd=nd):
                            r.need("tensor", s["SV"], nd)
                            r.need("tensor", s["SV"], st8["evT"])
                            for qs in range(4):
                                inc = (s["SP"], 1) if qs == 3 else None
                                r.do("tensor", lambda e, qs=qs: e.transpose(
                                    ps_T[:, qs * 128:(qs + 1) * 128],
                                    Otm[:, qs * 128:(qs + 1) * 128],
                                    t["ident_bf"][:]), inc=inc)
                            tm = s["SP"].n
                            r.need("vector", s["SP"], tm)
                            st8["evT"] = r.do(
                                "vector", lambda e, h=h, qg=qg: e.tensor_copy(
                                    t["bigA"][:, h * 2048 + qg * 512:
                                              h * 2048 + (qg + 1) * 512],
                                    ps_T[:]), inc=(s["SV"], 1))

                        st8["pend"] = flushT
                    st8["kq_free"][par] = s["SP"].n
                st8["v4_free"][vpar] = s["SP"].n
            if st8["pend"] is not None:
                st8["pend"]()
                st8["pend"] = None

    def oproj_phase(self, wsel, ready, brow_dram, src, dst):
        r, s, t = self.r, self.s, self.t
        self.entry_wait("tensor")
        for sem, val in ready:
            r.need("tensor", sem, val)
        nc = self.nc
        from contextlib import ExitStack
        with ExitStack() as _stk:
            _e = _stk.enter_context
            ps_p = [_e(nc.psum_tensor(self.nm("ps_o"), [128, 512], F32))
                    for _ in range(2)]
            brow = _e(nc.sbuf_tensor(self.nm("brow"), [1, 2048], BF16))
            ones_row = _e(nc.sbuf_tensor(self.nm("onr"), [1, 128], BF16))
            brow_ap = brow[:]
            r.need("gpsimd", s["SP"])
            bm = r.do("gpsimd", lambda e, brow=brow, brow_dram=brow_dram:
                      e.dma_start(brow[:], brow_dram), inc=(s["SCNW"], 16))
            r.need("tensor", s["SCNW"], bm)
            r.need("vector", s["SA"])
            r.do("vector", lambda e, ones_row=ones_row: e.memset(
                ones_row[:], 1.0), inc=(s["SV"], 1))
            r.need("tensor", s["SV"])
            xtile = [_e(nc.sbuf_tensor(self.nm("oxt"), [128, 2048], F32))
                     for _ in range(2)]
            xnew = [_e(nc.sbuf_tensor(self.nm("oxn"), [128, 2048], F32))
                    for _ in range(2)]
            ps_free = [0, 0]
            x_free = [0, 0]
            xn_free = [0, 0]
            for st in range(NST):
                par = st % 2
                xb = xtile[par]
                sxl = s["SXL"][par]
                r.need("sync", s["SV"], x_free[par])
                xl = r.do("sync", lambda e, xb=xb, st=st, src=src: e.dma_start(
                    xb[:], src[st * 128:(st + 1) * 128, :]), inc=(sxl, 16))
                xn = xnew[par]
                sxn = s["SXN"][par]
                r.need("vector", sxn, xn_free[par])
                for dt_ in range(4):
                    ps = ps_p[dt_ % 2]
                    r.need("tensor", s["SV"], ps_free[dt_ % 2])
                    for f in range(DC):
                        wap = wsel(f, dt_ * 512, (dt_ + 1) * 512)
                        r.do("tensor", lambda e, ps=ps, f=f, st=st, wap=wap:
                             e.matmul(ps[:],
                                      t["bigA"][:, f * 2048 + st * 128:
                                                f * 2048 + (st + 1) * 128],
                                      wap,
                                      start=(f == 0), stop=False))
                    r.do("tensor", lambda e, ps=ps, dt_=dt_, brow_ap=brow_ap,
                         ones_row=ones_row:
                         e.matmul(ps[:], ones_row[:],
                                  brow_ap[:, dt_ * 512:(dt_ + 1) * 512],
                                  start=False, stop=True), inc=(s["SP"], 1))
                    mm = s["SP"].n
                    r.need("vector", s["SP"], mm)
                    r.need("vector", sxl, xl)
                    ev = r.do("vector", lambda e, xn=xn, ps=ps, xb=xb, dt_=dt_:
                              e.tensor_tensor(xn[:, dt_ * 512:(dt_ + 1) * 512],
                                              ps[:],
                                              xb[:, dt_ * 512:(dt_ + 1) * 512],
                                              ALU.add), inc=(s["SV"], 1))
                    ps_free[dt_ % 2] = ev
                x_free[par] = s["SV"].n
                r.need("sync", s["SV"], s["SV"].n)
                stm = r.do("sync", lambda e, xn=xn, st=st, dst=dst: e.dma_start(
                    dst[st * 128:(st + 1) * 128, :], xn[:]), inc=(sxn, 16))
                xn_free[par] = stm
        return [(s["SXN"][i], s["SXN"][i].n) for i in range(2)]

    def ffn_phase(self, wsel, ready, src, dst, absr, gmarks):
        r, s, t = self.r, self.s, self.t
        self.entry_wait("tensor")
        for sem, val in ready:
            r.need("tensor", sem, val)
        a_sb, b_sb, r_sb = absr
        nc = self.nc
        from contextlib import ExitStack
        with ExitStack() as _stk:
            _e = _stk.enter_context
            ps_r = _e(nc.psum_tensor(self.nm("ps_r"), [128, 8], F32))
            ps_wdT = _e(nc.psum_tensor(self.nm("ps_wdT"), [8, 128], BF16))
            ps_wrep = _e(nc.psum_tensor(self.nm("ps_wrp"), [128, 128], F32))
            ps_ax = _e(nc.psum_tensor(self.nm("ps_ax"), [128, 128], F32))
            ps_f = [_e(nc.psum_tensor(self.nm("ps_f"), [128, 512], F32))
                    for _ in range(4)]
            xtile = [_e(nc.sbuf_tensor(self.nm("fxt"), [128, 2048], F32))
                     for _ in range(2)]
            xnew = [_e(nc.sbuf_tensor(self.nm("fxn"), [128, 2048], F32))]
            top = _e(nc.sbuf_tensor(self.nm("top"), [128, 8], F32))
            t1s = _e(nc.sbuf_tensor(self.nm("t1s"), [128, 8], F32))
            t2s = _e(nc.sbuf_tensor(self.nm("t2s"), [128, 8], F32))
            mskd = _e(nc.sbuf_tensor(self.nm("mskd"), [128, 8], F32))
            td1 = _e(nc.sbuf_tensor(self.nm("td1"), [128, 8], F32))
            wdense = _e(nc.sbuf_tensor(self.nm("wd"), [128, 8], BF16))
            wdT_sb = _e(nc.sbuf_tensor(self.nm("wdT"), [8, 128], BF16))
            wrep_sb = _e(nc.sbuf_tensor(self.nm("wrp"), [128, 128], BF16))
            g_sb = [_e(nc.sbuf_tensor(self.nm("gsb"), [128, 128], BF16))
                    for _ in range(2)]
            exp8 = _e(nc.sbuf_tensor(self.nm("exp8"), [8, 128], BF16))
            r.need("gpsimd", s["SP"])
            e8m = r.do("gpsimd", lambda e, exp8=exp8: e.dma_start(
                exp8[:], t["exp8_dram"]), inc=(s["SCNW"], 16))
            r.need("tensor", s["SCNW"], e8m)
            ps_free = [0, 0, 0, 0]
            x_free = [0, 0]
            xn_free = [0]
            st9 = {"g_free": [0, 0], "r_read": 0, "wdt_free": 0,
                   "wrep_free": 0, "ax_free": 0, "wdense_free": 0,
                   "g_mark": [0, 0], "xl": [0, 0]}

            def x_load(n):
                par = n % 2
                xb = xtile[par]
                sxl = s["SXL"][par]
                r.need("sync", s["SV"], x_free[par])
                st9["xl"][par] = r.do(
                    "sync", lambda e, xb=xb, n=n, src=src: e.dma_start(
                        xb[:], src[n * 128:(n + 1) * 128, :]), inc=(sxl, 16))

            def router_ax(n):
                r.need("tensor", s["SV"], gmarks[n // 4])
                r.need("tensor", s["SV"], st9["r_read"])
                for dc in range(DC):
                    r.do("tensor", lambda e, dc=dc, n=n: e.matmul(
                        ps_r[:],
                        t["bigA"][:, dc * 2048 + n * 128:
                                  dc * 2048 + (n + 1) * 128],
                        r_sb[:, dc * 8:(dc + 1) * 8],
                        start=(dc == 0), stop=(dc == DC - 1)),
                        inc=(s["SP"], 1) if dc == DC - 1 else None)
                rmm = s["SP"].n
                r.need("vector", s["SP"], rmm)
                r.do("vector", lambda e: e.reduce_max(
                    top[:, 0:1], ps_r[:], axis=AX), inc=(s["SV"], 1))
                r.need("vector", s["SV"])
                r.do("vector", lambda e: e.tensor_scalar(
                    t1s[:], ps_r[:], top[:, 0:1], None, ALU.is_equal),
                    inc=(s["SV"], 1))
                r.need("vector", s["SV"])
                r.do("vector", lambda e: e.scalar_tensor_tensor(
                    mskd[:], t1s[:], -1e30, ps_r[:], ALU.mult, ALU.add),
                    inc=(s["SV"], 1))
                r.need("vector", s["SV"])
                r.do("vector", lambda e: e.reduce_max(
                    top[:, 1:2], mskd[:], axis=AX), inc=(s["SV"], 1))
                r.need("vector", s["SV"])
                r.do("vector", lambda e: e.tensor_scalar(
                    t2s[:], mskd[:], top[:, 1:2], None, ALU.is_equal),
                    inc=(s["SV"], 1))
                r.do("vector", lambda e: e.tensor_sub(
                    top[:, 2:3], top[:, 1:2], top[:, 0:1]), inc=(s["SV"], 1))
                r.need("scalar", s["SV"])
                r.do("scalar", lambda e: e.activation(
                    top[:, 3:4], top[:, 2:3], AF.Exp), inc=(s["SA"], 1))
                r.need("vector", s["SA"])
                r.do("vector", lambda e: e.tensor_scalar_add(
                    top[:, 4:5], top[:, 3:4], 1.0), inc=(s["SV"], 1))
                r.need("vector", s["SV"])
                r.do("vector", lambda e: e.reciprocal(
                    top[:, 5:6], top[:, 4:5]), inc=(s["SV"], 1))
                r.need("vector", s["SV"])
                r.do("vector", lambda e: e.tensor_mul(
                    top[:, 6:7], top[:, 3:4], top[:, 5:6]), inc=(s["SV"], 1))
                r.do("vector", lambda e: e.tensor_scalar(
                    td1[:], t1s[:], top[:, 5:6], None, ALU.mult),
                    inc=(s["SV"], 1))
                r.need("vector", s["SV"])
                r.need("vector", s["SP"], st9["wdense_free"])
                r.do("vector", lambda e: e.scalar_tensor_tensor(
                    wdense[:], t2s[:], top[:, 6:7], td1[:],
                    ALU.mult, ALU.add), inc=(s["SV"], 1))
                st9["r_read"] = s["SV"].n

            def gating_pe(n):
                r.need("tensor", s["SV"], st9["ax_free"])
                for dc in range(DC):
                    r.do("tensor", lambda e, dc=dc, n=n: e.matmul(
                        ps_ax[:],
                        a_sb[:, dc * 128:(dc + 1) * 128],
                        t["bigA"][:, dc * 2048 + n * 128:
                                  dc * 2048 + (n + 1) * 128],
                        start=(dc == 0), stop=(dc == DC - 1)),
                        inc=(s["SP"], 1) if dc == DC - 1 else None)
                wd = st9["r_read"]
                r.need("tensor", s["SV"], wd)
                r.need("tensor", s["SV"], st9["wdt_free"])
                r.do("tensor", lambda e: e.transpose(
                    ps_wdT[:], wdense[:], t["ident_bf"][:]), inc=(s["SP"], 1))
                st9["wdense_free"] = s["SP"].n
                r.need("vector", s["SP"], s["SP"].n)
                r.do("vector", lambda e: e.tensor_copy(wdT_sb[:], ps_wdT[:]),
                     inc=(s["SV"], 1))
                st9["wdt_free"] = s["SV"].n
                r.need("tensor", s["SV"], st9["wdt_free"])
                r.need("tensor", s["SV"], st9["wrep_free"])
                r.do("tensor", lambda e: e.matmul(
                    ps_wrep[:], exp8[:], wdT_sb[:],
                    start=True, stop=True), inc=(s["SP"], 1))
                wrepm = s["SP"].n
                r.need("vector", s["SP"], wrepm)
                r.do("vector", lambda e: e.tensor_copy(wrep_sb[:], ps_wrep[:]),
                     inc=(s["SV"], 1))
                st9["wrep_free"] = s["SV"].n
                gp = n % 2
                gsb = g_sb[gp]
                r.need("vector", s["SV"], s["SV"].n)
                r.need("vector", s["SP"], st9["g_free"][gp])
                r.do("vector", lambda e, gsb=gsb: e.tensor_tensor(
                    gsb[:], ps_ax[:], wrep_sb[:], ALU.mult), inc=(s["SV"], 1))
                st9["g_mark"][gp] = s["SV"].n
                st9["ax_free"] = s["SV"].n

            x_load(0)
            router_ax(0)
            gating_pe(0)
            for n in range(NST):
                par = n % 2
                xb = xtile[par]
                if n + 1 < NST:
                    x_load(n + 1)
                    router_ax(n + 1)
                gp = n % 2
                gsb = g_sb[gp]
                xn = xnew[0]
                sxn = s["SXN"][0]
                r.need("vector", sxn, xn_free[0])
                for ot in range(4):
                    ps = ps_f[ot]
                    r.need("tensor", s["SV"], ps_free[ot])
                    for dc in range(DC):
                        wap = wsel(dc, ot * 512, (ot + 1) * 512)
                        r.do("tensor", lambda e, ps=ps, dc=dc, n=n, wap=wap:
                             e.matmul(ps[:],
                                      t["bigA"][:, dc * 2048 + n * 128:
                                                dc * 2048 + (n + 1) * 128],
                                      wap,
                                      start=(dc == 0), stop=False))
                    r.need("tensor", s["SV"], st9["g_mark"][gp])
                    r.do("tensor", lambda e, ps=ps, gsb=gsb, ot=ot: e.matmul(
                        ps[:], gsb[:], b_sb[:, ot * 512:(ot + 1) * 512],
                        start=False, stop=True), inc=(s["SP"], 1))
                    mm = s["SP"].n
                    r.need("vector", s["SP"], mm)
                    r.need("vector", s["SXL"][par], st9["xl"][par])
                    ev = r.do("vector", lambda e, xn=xn, ps=ps, xb=xb, ot=ot:
                              e.tensor_tensor(xn[:, ot * 512:(ot + 1) * 512],
                                              ps[:],
                                              xb[:, ot * 512:(ot + 1) * 512],
                                              ALU.add), inc=(s["SV"], 1))
                    ps_free[ot] = ev
                st9["g_free"][gp] = s["SP"].n
                if n + 1 < NST:
                    gating_pe(n + 1)
                x_free[par] = s["SV"].n
                r.need("sync", s["SV"], s["SV"].n)
                stm = r.do("sync", lambda e, xn=xn, n=n, dst=dst: e.dma_start(
                    dst[n * 128:(n + 1) * 128, :], xn[:]), inc=(sxn, 16))
                xn_free[0] = stm
        return [(s["SXN"][i], s["SXN"][i].n) for i in range(2)]



def build_nc(n_sub=4, dbg=False):
    nc = bass.Bass("TRN2", target_bir_lowering=False)
    dram_in = {}

    def din(name, shape):
        dram_in[name] = nc.dram_tensor(name, list(shape), F32,
                                       kind="ExternalInput").ap()
        return dram_in[name]

    x_in = din("x_own", (NTOK, D))
    for nm_ in ("aw_q", "aw_k", "aw_v", "aw_o", "bw_q", "bw_k", "bw_v", "bw_o",
                "fw_t"):
        din(nm_, (128, DC * 2048))
    for nm_ in ("as_q", "as_k", "as_v", "as_o"):
        din(nm_, (DC, 4, 2048))
    din("ab_qk", (128, 32))
    din("ab_vrow", (1, 2048))
    din("ab_orow", (1, 2048))
    din("bb_qk", (128, 32))
    din("bb_vrow", (1, 2048))
    din("bb_orow", (1, 2048))
    din("fw_a", (128, 2048))
    din("fw_b", (128, 2048))
    din("fw_r", (128, 128))
    din("c_ident", (128, 128))
    din("c_msec0", (128, 128))
    din("c_msec1", (128, 128))
    din("c_exp8", (8, 128))

    y = nc.dram_tensor("y_own", [NTOK, D], F32, kind="ExternalOutput").ap()
    x_cur = nc.dram_tensor("x_cur", [NTOK, D], F32, kind="Internal").ap()
    q_dram = nc.dram_tensor("q_dram", [D, NTOK], BF16, kind="Internal").ap()
    kv_own = nc.dram_tensor("kv_own", [2 * NTOK, D], BF16, kind="Internal")
    kvf_K = nc.dram_tensor("kvf_K", [2 * NTOK, D], BF16, kind="Internal")
    kvf_V = nc.dram_tensor("kvf_V", [2 * NTOK, D], BF16, kind="Internal")

    rec = Rec()
    from contextlib import ExitStack
    with ExitStack() as stack:
        e = stack.enter_context
        sems = {}
        for nm_ in ("SP", "SV", "SA", "SCK", "SCV", "SWC", "SCN", "SCNW"):
            sems[nm_] = Sem(e(nc.semaphore(nm_)), nm_)
        sems["SV4"] = [Sem(e(nc.semaphore(f"SV4{i}")), f"SV4{i}")
                       for i in range(2)]
        for nm_, cnt in (("SXL", 2), ("SXN", 2), ("SSR", 2), ("SKQ", 2),
                         ("STG", 4), ("SWH", 3), ("SQG", 2)):
            sems[nm_] = [Sem(e(nc.semaphore(f"{nm_}{i}")), f"{nm_}{i}")
                         for i in range(cnt)]

        t = {}
        t["bigA"] = e(nc.sbuf_tensor("bigA", [128, DC * 2048], BF16))
        t["bigW"] = e(nc.sbuf_tensor("bigW", [128, DC * 2048], BF16))
        t["pbuf"] = e(nc.sbuf_tensor("pbuf", [128, 8 * 2048], BF16))
        t["srep"] = [e(nc.sbuf_tensor(f"srep{i}", [128, 2048], BF16))
                     for i in range(2)]
        t["ident_bf"] = e(nc.sbuf_tensor("ident_bf", [128, 128], BF16))
        t["mask_sec"] = [e(nc.sbuf_tensor(f"msec{i}", [128, 128], BF16))
                         for i in range(2)]
        t["ones_col"] = e(nc.sbuf_tensor("ones_col", [128, 1], BF16))
        t["q_dram"] = q_dram
        t["exp8_dram"] = dram_in["c_exp8"]
        t["ident_dram"] = dram_in["c_ident"]
        t["kvf_K"] = kvf_K.ap()
        t["kvf_V"] = kvf_V.ap()

        bld = Build(nc, rec, sems, t)
        r, s = rec, sems

        r.do("gpsimd", lambda e: e.dma_start(t["ident_bf"][:],
                                             dram_in["c_ident"]),
             inc=(s["SCNW"], 16))
        r.do("gpsimd", lambda e: e.dma_start(t["mask_sec"][0][:],
                                             dram_in["c_msec0"]),
             inc=(s["SCNW"], 16))
        r.do("gpsimd", lambda e: e.dma_start(t["mask_sec"][1][:],
                                             dram_in["c_msec1"]),
             inc=(s["SCNW"], 16))
        r.do("vector", lambda e: e.memset(t["ones_col"][:], 1.0),
             inc=(s["SV"], 1))
        r.need("tensor", s["SCN"])
        r.need("tensor", s["SCNW"])
        r.need("tensor", s["SV"])
        r.need("scalar", s["SCN"])
        r.need("scalar", s["SCNW"])
        r.need("vector", s["SCN"])
        r.need("vector", s["SCNW"])

        def load_biases(pref, b_qk):
            r.need("sync", s["SP"])
            r.do("sync", lambda e, b_qk=b_qk: e.dma_start(
                b_qk[:], dram_in[pref + "b_qk"]), inc=(s["SCN"], 16))
            return s["SCN"].n

        def attn_core(pref, is_a, src):
            wp = "a" if is_a else "b"
            with ExitStack() as _as:
                _ae = _as.enter_context
                ps_p = [_ae(nc.psum_tensor(bld.nm("ps_p"), [128, 512], F32))
                        for _ in range(3)]
                b_qk = _ae(nc.sbuf_tensor(bld.nm("bqk"), [128, 32], F32))
                stage = [_ae(nc.sbuf_tensor(bld.nm("sgp"), [128, 512], BF16))
                         for _ in range(4)]
                t["b_qk"] = b_qk
                share = {"ps": [None, None, None], "stg": [0, 0, 0, 0]}
                wsel, ready, ab = bld.load_weight(
                    dram_in[wp + "w_k"], dram_in["as_k"] if is_a else None)
                bias_m = load_biases(pref, b_qk)
                gmarks = bld.ln_phase(src)
                ln_sa, ln_sv = s["SA"].n, s["SV"].n
                r.need("scalar", s["SCN"], bias_m)
                r.need("tensor", s["SCN"], bias_m)
                bld.proj_fm(ps_p, stage, gmarks, wsel, ready,
                            t["b_qk"][:, 16:32], kv_own.ap(), 0, share)
                bld.wring_release(ab)
                wsel, ready, ab = bld.load_weight(
                    dram_in[wp + "w_v"], dram_in["as_v"] if is_a else None)
                for sg in s["STG"]:
                    r.need("gpsimd", sg, sg.n)
                for ci in range(4):
                    r.do("gpsimd", lambda e, ci=ci: e.collective_compute(
                        "AllGather", ALU.bypass,
                        replica_groups=[[0, 1], [2, 3], [4, 5], [6, 7]],
                        ins=[kv_own.ap()[ci * 512:(ci + 1) * 512, :].opt()],
                        outs=[kvf_K.ap()[ci * 1024:(ci + 1) * 1024, :].opt()]),
                        inc=(s["SCK"], 1))
                bld.proj_tm(ps_p, stage, gmarks, wsel, ready,
                            dram_in[pref + "b_vrow"], kv_own.ap(), NTOK,
                            share, (ln_sa, ln_sv))
                bld.wring_release(ab)
                wsel, ready, ab = bld.load_weight(
                    dram_in[wp + "w_q"], dram_in["as_q"] if is_a else None)
                for sg in s["STG"]:
                    r.need("gpsimd", sg, sg.n)
                for ci in range(4):
                    r.do("gpsimd", lambda e, ci=ci: e.collective_compute(
                        "AllGather", ALU.bypass,
                        replica_groups=[[0, 1], [2, 3], [4, 5], [6, 7]],
                        ins=[kv_own.ap()[NTOK + ci * 512:
                                         NTOK + (ci + 1) * 512, :].opt()],
                        outs=[kvf_V.ap()[ci * 1024:(ci + 1) * 1024, :].opt()]),
                        inc=(s["SCV"], 1))
                q_marks = bld.proj_fm(ps_p, stage, gmarks, wsel, ready,
                                      t["b_qk"][:, 0:16], q_dram, 0, share)
                bld.wring_release(ab)
            wo_sel, wo_ready, wo_ab = bld.load_weight(
                dram_in[wp + "w_o"], dram_in["as_o"] if is_a else None)
            halves = [(t["bigW"], 0), (t["bigW"], 8 * 2048), (t["pbuf"], 0)]
            fh = 3 - wo_ab[0] - wo_ab[1]
            ften, foff = halves[fh]
            v2b_ap = ften[:, foff:foff + 8192]
            bld.attention(not is_a, q_marks, v2b_ap)
            bld.wring_free[fh] = s["SP"].n
            return wo_sel, wo_ready, wo_ab

        def attn_sublayer(pref, is_a, src, dst):
            wo_sel, wo_ready, wo_ab = attn_core(pref, is_a, src)
            bld.barrier()
            out_marks = bld.oproj_phase(wo_sel, wo_ready,
                                        dram_in[pref + "b_orow"], src, dst)
            bld.wring_release(wo_ab)
            bld.barrier()
            return out_marks

        def ffn_sublayer(src, dst):
            with ExitStack() as _fs:
                a_sb = _fs.enter_context(
                    nc.sbuf_tensor(bld.nm("a_sb"), [128, 2048], BF16))
                b_sb = _fs.enter_context(
                    nc.sbuf_tensor(bld.nm("b_sb"), [128, 2048], BF16))
                r_sb = _fs.enter_context(
                    nc.sbuf_tensor(bld.nm("r_sb"), [128, 128], BF16))
                wsel, ready, ab = bld.load_weight(dram_in["fw_t"], None)
                r.need("gpsimd", s["SP"])
                r.do("gpsimd", lambda e, a_sb=a_sb: e.dma_start(
                    a_sb[:], dram_in["fw_a"]), inc=(s["SWC"], 16))
                r.do("gpsimd", lambda e, b_sb=b_sb: e.dma_start(
                    b_sb[:], dram_in["fw_b"]), inc=(s["SWC"], 16))
                m3 = r.do("gpsimd", lambda e, r_sb=r_sb: e.dma_start(
                    r_sb[:], dram_in["fw_r"]), inc=(s["SWC"], 16))
                gmarks = bld.ln_phase(src)
                ready2 = list(ready) + [(s["SWC"], m3)]
                out_marks = bld.ffn_phase(wsel, ready2, src, dst,
                                          (a_sb, b_sb, r_sb), gmarks)
                bld.wring_release(ab)
                bld.barrier()
            return out_marks

        final_marks = []
        if n_sub < 0:
            bld.ln_phase(x_in)
            _dbstk = ExitStack()
            if n_sub <= -2:
                with nc.psum_tensor(bld.nm("ps_p0"), [128, 512], F32) as pp0, \
                        nc.psum_tensor(bld.nm("ps_p1"), [128, 512], F32) as pp1, \
                        nc.psum_tensor(bld.nm("ps_p2"), [128, 512], F32) as pp2:
                    ps_p = [pp0, pp1, pp2]
                    b_qk = _dbstk.enter_context(
                        nc.sbuf_tensor(bld.nm("bqk"), [128, 32], F32))
                    stage = [_dbstk.enter_context(nc.sbuf_tensor(
                        bld.nm("sgp"), [128, 512], BF16)) for _ in range(4)]
                    t["b_qk"] = b_qk
                    bias_m = load_biases("a", b_qk)
                    wsel, ready, ab = bld.load_weight(dram_in["aw_k"],
                                                      dram_in["as_k"])
                    gmarks = [s["SV"].n] * 4
                    r.need("scalar", s["SCN"], bias_m)
                    r.need("tensor", s["SCN"], bias_m)
                    share = {"ps": [None, None, None], "stg": [0, 0, 0, 0]}
                    bld.proj_fm(ps_p, stage, gmarks, wsel, ready,
                                t["b_qk"][:, 16:32], kv_own.ap(), 0, share)
                    bld.wring_release(ab)
                    if n_sub <= -3:
                        wsel, ready, ab = bld.load_weight(dram_in["aw_v"],
                                                          dram_in["as_v"])
                        bld.proj_tm(ps_p, stage, gmarks, wsel, ready,
                                    dram_in["ab_vrow"], kv_own.ap(), NTOK,
                                    share)
                        bld.wring_release(ab)
            bld.barrier()

        seq = []
        if n_sub >= 1:
            seq.append(("attn", "a", x_in, x_cur if n_sub > 1 else y))
        if n_sub >= 2:
            seq.append(("ffn", None, x_cur, x_cur if n_sub > 2 else y))
        if n_sub >= 3:
            seq.append(("attn", "b", x_cur, x_cur if n_sub > 3 else y))
        if n_sub >= 4:
            seq.append(("ffn", None, x_cur, y))

        for kind, which, src, dst in seq:
            if kind == "attn":
                final_marks = attn_sublayer(which, which == "a", src, dst)
            else:
                final_marks = ffn_sublayer(src, dst)

        for sem, val in final_marks:
            r.need("sync", sem, val)

        if dbg:
            dq = nc.dram_tensor("dbg_q", [D, NTOK], BF16,
                                kind="ExternalOutput").ap()
            dkv = nc.dram_tensor("dbg_kv", [2 * NTOK, D], BF16,
                                 kind="ExternalOutput").ap()
            dkvf = nc.dram_tensor("dbg_kvf", [4 * NTOK, D], BF16,
                                  kind="ExternalOutput").ap()
            dba = nc.dram_tensor("dbg_bigA", [128, DC * 2048], BF16,
                                 kind="ExternalOutput").ap()
            dbw = nc.dram_tensor("dbg_bigW", [128, DC * 2048], BF16,
                                 kind="ExternalOutput").ap()
            bld.barrier()
            r.do("sync", lambda e: e.dma_start(dq, q_dram), inc=(s["SCN"], 16))
            r.do("sync", lambda e: e.dma_start(dkv, kv_own.ap()),
                 inc=(s["SCN"], 16))
            r.do("sync", lambda e: e.dma_start(dkvf, kv_full.ap()),
                 inc=(s["SCN"], 16))
            r.do("sync", lambda e: e.dma_start(dba, t["bigA"][:]),
                 inc=(s["SCN"], 16))
            r.do("sync", lambda e: e.dma_start(dbw, t["bigW"][:]),
                 inc=(s["SCN"], 16))
            r.need("sync", s["SCN"])

        with nc.Block() as block:
            @block.sync
            def _(eng):
                for fn in rec.q["sync"]:
                    fn(eng)

            @block.tensor
            def _(eng):
                for fn in rec.q["tensor"]:
                    fn(eng)

            @block.vector
            def _(eng):
                for fn in rec.q["vector"]:
                    fn(eng)

            @block.scalar
            def _(eng):
                for fn in rec.q["scalar"]:
                    fn(eng)

            @block.gpsimd
            def _(eng):
                for fn in rec.q["gpsimd"]:
                    fn(eng)

    return nc



def _chunk_interleave(w):
    return np.ascontiguousarray(
        w.reshape(DC, 128, 2048).transpose(1, 0, 2).reshape(128, DC * 2048))


def _bias_pack(b):
    return np.ascontiguousarray(np.asarray(b, np.float32).reshape(DC, 128).T)


def make_core_inputs(core, I):
    b, p = core // 2, core % 2
    rows = np.arange(NTOK)
    gidx = (2 * (rows // 128) + p) * 128 + rows % 128
    x = np.ascontiguousarray(np.asarray(I["L0_attn_a_x"], np.float32)[b][gidx])

    d = {"x_own": x}
    for nm_, key in (("q", "wq"), ("k", "wk"), ("v", "wv"), ("o", "wo")):
        d[f"aw_{nm_}"] = _chunk_interleave(
            np.asarray(I[f"L0_attn_a_{key}_codes"], np.float32))
        d[f"as_{nm_}"] = np.ascontiguousarray(
            np.asarray(I[f"L0_attn_a_{key}_scales"], np.float32)
            .reshape(DC, 4, 2048))
    d["ab_qk"] = np.concatenate(
        [_bias_pack(I["L0_attn_a_bq"]), _bias_pack(I["L0_attn_a_bk"])], axis=1)
    d["ab_vrow"] = np.asarray(I["L0_attn_a_bv"], np.float32).reshape(1, 2048)
    d["ab_orow"] = np.asarray(I["L0_attn_a_bo"], np.float32).reshape(1, 2048)
    for nm_, key in (("q", "Wq"), ("k", "Wk"), ("v", "Wv")):
        w = np.asarray(I[f"L0_attn_b_{key}"], np.float32).transpose(1, 0, 2)
        d[f"bw_{nm_}"] = _chunk_interleave(
            np.ascontiguousarray(w.reshape(D, 2048)))
    d["bw_o"] = _chunk_interleave(np.asarray(I["L0_attn_b_Wo"], np.float32))
    d["bb_qk"] = np.concatenate(
        [_bias_pack(np.asarray(I["L0_attn_b_bq"], np.float32).reshape(-1)),
         _bias_pack(np.asarray(I["L0_attn_b_bk"], np.float32).reshape(-1))],
        axis=1)
    d["bb_vrow"] = np.asarray(I["L0_attn_b_bv"], np.float32).reshape(1, 2048)
    d["bb_orow"] = np.asarray(I["L0_attn_b_bo"], np.float32).reshape(1, 2048)
    d["fw_t"] = _chunk_interleave(
        np.ascontiguousarray(np.asarray(I["L0_ffn_weight"], np.float32).T))
    d["fw_a"] = np.ascontiguousarray(
        np.asarray(I["L0_ffn_lora_A"], np.float32).transpose(2, 0, 1)
        .reshape(D, 128).reshape(DC, 128, 128).transpose(1, 0, 2)
        .reshape(128, 2048))
    d["fw_b"] = np.ascontiguousarray(
        np.asarray(I["L0_ffn_lora_B"], np.float32).transpose(0, 2, 1)
        .reshape(128, D))
    d["fw_r"] = np.ascontiguousarray(
        np.asarray(I["L0_ffn_router_w"], np.float32).T.reshape(DC, 128, 8)
        .transpose(1, 0, 2).reshape(128, 128))
    d["c_ident"] = np.eye(128, dtype=np.float32)
    tri = np.triu(np.ones((128, 128), np.float32))
    bound = (np.ones if p == 1 else np.zeros)((128, 128), np.float32)
    d["c_msec0"] = tri if p == 0 else bound
    d["c_msec1"] = bound if p == 0 else tri
    ex = np.zeros((8, 128), np.float32)
    for e_ in range(8):
        ex[e_, e_ * 16:(e_ + 1) * 16] = 2.0
    d["c_exp8"] = ex
    return d


_NC_CACHE = {}


def kernel(**inputs):
    n_sub = int(inputs.pop("_n_sub", 4))
    dbg = bool(inputs.pop("_dbg", False))
    key = (n_sub, dbg)
    if key not in _NC_CACHE:
        _NC_CACHE[key] = build_nc(n_sub, dbg)
    nc = _NC_CACHE[key]
    in_maps = [make_core_inputs(c, inputs) for c in range(NCORES)]
    res = run_bass_kernel_spmd(nc, in_maps, core_ids=list(range(NCORES)))
    if dbg:
        kernel._dbg_res = res
    out = np.zeros((B, S, D), np.float32)
    for c in range(NCORES):
        b, p = c // 2, c % 2
        rows = np.arange(NTOK)
        gidx = (2 * (rows // 128) + p) * 128 + rows % 128
        out[b][gidx] = np.asarray(res.results[c]["y_own"], np.float32)
    return out


if __name__ == "__main__":
    nc = build_nc(int(sys.argv[1]) if len(sys.argv) > 1 else 4)
    print("built ok")
